# version 13
# baseline (speedup 1.0000x reference)
from contextlib import ExitStack

import numpy as np
import ml_dtypes
import concourse.bass as bass
import concourse.mybir as mybir
from concourse.bass_utils import run_bass_kernel_spmd

F32 = mybir.dt.float32
BF16 = mybir.dt.bfloat16
I32 = mybir.dt.int32
AF = mybir.ActivationFunctionType
ALU = mybir.AluOpType
AX = mybir.AxisListType

T = 2048
D = 1024
SEQ_PER_CORE = 2
TOK = SEQ_PER_CORE * T
NT = TOK // 128
NTS = T // 128
NC = D // 128
EPS_LN = 1e-5
EPS_HYP = 1e-6
K_CAP = 1536
NHEAD = 16
DH = 64
BISECT_ITERS = 36


NDMASTREAM = 4


class Node:
    __slots__ = ("eng", "emit", "deps", "is_dma", "needed", "val", "graph",
                 "stream")

    def __init__(self, eng, emit, deps, is_dma, graph):
        self.eng = eng
        self.emit = emit
        self.deps = [d for d in deps if d is not None]
        self.is_dma = is_dma
        self.needed = False
        self.val = None
        self.graph = graph
        self.stream = None


class Graph:
    CENG = ("tensor", "vector", "scalar", "gpsimd", "sync")

    def __init__(self, nc, sems, counts):
        self.nc = nc
        self.sems = sems
        self.counts = counts
        self.nodes = {e: [] for e in self.CENG}
        self.lowered = False
        self.counts0 = dict(counts)

    def add(self, eng, emit, deps=(), dma=False):
        n = Node(eng, emit, deps, dma, self)
        self.nodes[eng].append(n)
        return n

    def dma(self, eng, out, in_, deps=(), **kw):
        return self.add(eng, lambda E: E.dma_start(out, in_, **kw), deps, dma=True)

    def lower(self, drain_gpsimd=True):
        for e in self.CENG:
            for n in self.nodes[e]:
                for d in n.deps:
                    assert d.graph.lowered or d.graph is self, \
                        "dep on a LATER graph's node"
                    d.needed = True
        for e in self.CENG:
            always = e in ("vector", "scalar", "gpsimd")
            for n in self.nodes[e]:
                if n.is_dma:
                    s = self.counts["rr_" + e] % NDMASTREAM
                    self.counts["rr_" + e] += 1
                    key = f"d_{e}{s}"
                    n.stream = key
                    self.counts[key] += 16
                    n.val = self.counts[key]
                elif n.needed or always:
                    self.counts["c_" + e] += 1
                    n.val = self.counts["c_" + e]
        drains = []
        if drain_gpsimd:
            for s_ in range(NDMASTREAM):
                key = f"d_gpsimd{s_}"
                if self.counts[key] > self.counts0.get(key, 0):
                    drains.append((key, self.counts[key]))
        with self.nc.Block() as block:
            for e in self.CENG:
                if not self.nodes[e] and not (e == "sync" and drains):
                    continue

                def body(E, e=e):
                    hw = {}
                    for n in self.nodes[e]:
                        for d in n.deps:
                            key = d.stream if d.is_dma else ("c_" + d.eng)
                            assert d.val is not None, "dep on un-inc'd node"
                            if hw.get(key, 0) < d.val:
                                E.wait_ge(self.sems[key], d.val)
                                hw[key] = d.val
                        if n.is_dma and n.val > 16:
                            if hw.get(n.stream, 0) < n.val - 16:
                                E.wait_ge(self.sems[n.stream], n.val - 16)
                                hw[n.stream] = n.val - 16
                        inst = n.emit(E)
                        if n.is_dma:
                            inst.then_inc(self.sems[n.stream], 16)
                        elif n.val is not None:
                            inst.then_inc(self.sems["c_" + e], 1)
                    if e == "sync":
                        need_nop = not self.nodes[e]
                        for key, v in drains:
                            if hw.get(key, 0) < v:
                                E.wait_ge(self.sems[key], v)
                                hw[key] = v
                                need_nop = True
                        if need_nop:
                            E.nop()

                getattr(block, e)(body)
        self.lowered = True
        return self


def make_sems(nc, stack):
    sems = {}
    counts = {}
    for e in Graph.CENG:
        key = "c_" + e
        sems[key] = stack.enter_context(nc.semaphore(key))
        counts[key] = 0
        counts["rr_" + e] = 0
        for s in range(NDMASTREAM):
            key = f"d_{e}{s}"
            sems[key] = stack.enter_context(nc.semaphore(key))
            counts[key] = 0
    return sems, counts


def emit_consts(g, C):
    if "ident_bf" in C:
        n0 = g.add("gpsimd", lambda E: E.memset(C["ident_bf"], 0.0))
        g.add("gpsimd", lambda E: E.affine_select(
            out=C["ident_bf"], in_=C["ident_bf"], compare_op=ALU.not_equal,
            fill=1.0, base=0, pattern=[[-1, 128]], channel_multiplier=1), deps=[n0])
    if "ident_f" in C:
        n0 = g.add("gpsimd", lambda E: E.memset(C["ident_f"], 0.0))
        g.add("gpsimd", lambda E: E.affine_select(
            out=C["ident_f"], in_=C["ident_f"], compare_op=ALU.not_equal,
            fill=1.0, base=0, pattern=[[-1, 128]], channel_multiplier=1), deps=[n0])
    if "ones_col" in C:
        g.add("gpsimd", lambda E: E.memset(C["ones_col"], 1.0))
    if "ones_row" in C:
        g.add("gpsimd", lambda E: E.memset(C["ones_row"], 1.0))


def ln_stats_nodes(g, xt_ap, s1_ap, s2_ap, sqdump, deps, sq_war=None):
    n1 = g.add("vector", lambda E: E.reduce_sum(s1_ap, xt_ap, axis=AX.X), deps)
    nq = g.add("scalar", lambda E: E.activation(sqdump, xt_ap, AF.Square),
               list(deps) + ([sq_war] if sq_war else []))
    n2 = g.add("vector", lambda E: E.reduce_sum(s2_ap, sqdump, axis=AX.X), [nq])
    return n1, n2


def ln_chain_nodes(g, s1, s2, mu, rstd, scr0, scr1, deps, eps_ap, inv_n=1.0 / D):
    a = g.add("vector", lambda E: E.tensor_scalar_mul(mu, s1, inv_n), deps)
    b = g.add("vector", lambda E: E.tensor_scalar_mul(scr0, s2, inv_n), deps)
    c = g.add("vector", lambda E: E.tensor_mul(scr1, mu, mu), [a])
    d = g.add("vector", lambda E: E.tensor_sub(scr0, scr0, scr1), [b, c])
    e = g.add("scalar", lambda E: E.activation(scr1, scr0, AF.Ln, bias=eps_ap), [d])
    f = g.add("scalar", lambda E: E.activation(rstd, scr1, AF.Exp, scale=-0.5), [e])
    return a, f


def transpose_tile_nodes(g, ps_tr, src_bf, dst_ap, ident_bf, deps,
                         evac_eng="scalar", bank_free=None, n_chunks=8,
                         evac_deps=()):
    trs = []
    for c in range(n_chunks):
        dd = list(deps) + ([bank_free] if (bank_free is not None and c == 0) else [])
        trs.append(g.add("tensor", lambda E, c=c: E.transpose(
            ps_tr[:, c * 128:(c + 1) * 128], src_bf[:, c * 128:(c + 1) * 128],
            ident_bf), deps=dd if c == 0 else []))
    src = ps_tr[:, 0:n_chunks * 128]
    if len(dst_ap.shape) == 3:
        src = src.rearrange("p (c k) -> p c k", c=n_chunks)
    if evac_eng == "scalar":
        ev = g.add("scalar", lambda E: E.copy(dst_ap, src),
                   deps=[trs[-1]] + list(evac_deps))
    else:
        ev = g.add("vector", lambda E: E.tensor_copy(dst_ap, src),
                   deps=[trs[-1]] + list(evac_deps))
    return ev


def emit_l0ffn(nc, sems, counts, xin, xout, wr_ap, wb_ap, dbg=None):
    xin_t = xin.rearrange("(t p) d -> p t d", p=128)
    xout_t = xout.rearrange("(t p) d -> p t d", p=128)
    with ExitStack() as ex:
        sb = lambda nm, sh, dt: ex.enter_context(nc.sbuf_tensor(nm, sh, dt))[:]
        psa = lambda nm, sh, dt=F32: ex.enter_context(nc.psum_tensor(nm, sh, dt))[:]

        xt = sb("ff_xt", [128, NTS * 1024], F32)
        wb_sb = sb("ff_wb", [128, NC * 1024], BF16)
        wr_bc = sb("ff_wrb", [128, 1024], F32)
        ident_bf = sb("ff_id", [128, 128], BF16)
        ident_f = sb("ff_idf", [128, 128], F32)
        ones_row = sb("ff_onr", [1, 128], F32)
        ones_col = sb("ff_onc", [128, 1], F32)
        stats = sb("ff_st", [128, 16 * 8], F32)
        S = lambda k: stats[:, 16 * k:16 * (k + 1)]
        wrbar = sb("ff_wbar", [128, 1], F32)
        cpool = sb("ff_cp", [128, 2], F32)
        bis = sb("ff_bis", [1, 16], F32)
        BI = lambda k: bis[:, k:k + 1]
        tau_bc = sb("ff_tau", [128, 1], F32)
        mid_bc = sb("ff_mid", [128, 1], F32)
        sqdump = sb("ff_sqd", [128, 1024], F32)
        xw_tmp = sb("ff_xwt", [128, 1024], F32)
        gt = sb("ff_gt", [128, 16], F32)
        xn = [sb(f"ff_xn{i}", [128, 1024], F32) for i in range(2)]
        xnbf = [sb(f"ff_xnb{i}", [128, 1024], BF16) for i in range(2)]
        xf = [sb(f"ff_xf{i}", [128, 1024], BF16) for i in range(2)]
        p_sb = [sb(f"ff_p{i}", [128, 1024], F32) for i in range(2)]
        dd_sb = [sb(f"ff_dd{i}", [128, 1024], F32) for i in range(2)]
        m_col = sb("ff_mc", [128, 2], F32)
        out_sb = [sb(f"ff_o{i}", [128, 1024], F32) for i in range(2)]
        ps_tr = psa("ff_ptr", [128, 1024], BF16)
        ps_p = [psa("ff_pp0", [128, 512]), psa("ff_pp1", [128, 512])]
        ps_cnt = psa("ff_pcnt", [128, 512])
        C = dict(ident_bf=ident_bf, ident_f=ident_f, ones_row=ones_row,
                 ones_col=ones_col)

        g = Graph(nc, sems, counts)
        emit_consts(g, C)
        g.add("gpsimd", lambda E: E.memset(cpool[:, 0:1], EPS_LN))
        n_wb = g.dma("gpsimd", wb_sb.rearrange("p (c e) -> p c e", c=NC),
                     wb_ap.rearrange("(c p) e -> p c e", p=128))
        n_wr = g.dma("sync", wr_bc,
                     wr_ap.unsqueeze(0).partition_broadcast(128).squeeze(1))
        g.add("vector", lambda E: E.reduce_sum(wrbar, wr_bc, axis=AX.X), [n_wr])
        g.lower(drain_gpsimd=False)

        for s in range(SEQ_PER_CORE):
            g = Graph(nc, sems, counts)
            n_prev_xw = None
            n_prev_sq = None
            loads = []
            for i in range(NTS):
                t = s * NTS + i
                xt_i = xt[:, i * 1024:(i + 1) * 1024]
                n_l = g.dma("sync", xt_i, xin_t[:, t, :])
                loads.append(n_l)
                _, n_prev_sq = ln_stats_nodes(
                    g, xt_i, S(0)[:, i:i + 1], S(1)[:, i:i + 1], sqdump,
                    [n_l], sq_war=n_prev_sq)
                n_x1 = g.add("vector", lambda E, xt_i=xt_i: E.tensor_mul(
                    xw_tmp, xt_i, wr_bc),
                    [n_l, n_wr] + ([n_prev_xw] if n_prev_xw else []))
                n_prev_xw = g.add("vector", lambda E, i=i: E.reduce_sum(
                    S(4)[:, i:i + 1], xw_tmp, axis=AX.X), [n_x1])
            n_last_load = loads[-1]
            dep_all = [n_prev_xw, n_prev_sq] + loads
            n_mu, n_rstd = ln_chain_nodes(g, S(0), S(1), S(2), S(3), S(6), S(7),
                                          dep_all, cpool[:, 0:1])
            n_a = g.add("vector", lambda E: E.tensor_scalar_mul(S(6), S(2), wrbar),
                        [n_mu, n_rstd])
            n_b = g.add("vector", lambda E: E.tensor_sub(S(7), S(4), S(6)), [n_a])
            n_s = g.add("vector", lambda E: E.tensor_mul(S(5), S(7), S(3)),
                        [n_b, n_rstd])
            n_p1 = g.add("vector", lambda E: E.reduce_max(S(6)[:, 0:1], S(5),
                                                          axis=AX.X), [n_s])
            n_p2 = g.add("vector", lambda E: E.tensor_reduce(S(6)[:, 1:2], S(5),
                                                             axis=AX.X, op=ALU.min),
                        [n_s])
            n_t1 = g.add("tensor", lambda E: E.transpose(
                ps_cnt[0:1, 0:128], S(6)[:, 0:1], ident_f), [n_p1, n_p2])
            n_t2 = g.add("tensor", lambda E: E.transpose(
                ps_cnt[0:1, 128:256], S(6)[:, 1:2], ident_f), [n_p1, n_p2])
            n_mx = g.add("vector", lambda E: E.reduce_max(BI(0), ps_cnt[0:1, 0:128],
                                                          axis=AX.X), [n_t1, n_t2])
            n_mn = g.add("vector", lambda E: E.tensor_reduce(
                BI(1), ps_cnt[0:1, 128:256], axis=AX.X, op=ALU.min), [n_t1, n_t2])
            n_lo = g.add("vector", lambda E: E.tensor_scalar_add(BI(2), BI(1), -1.0),
                         [n_mn])
            n_hi = g.add("vector", lambda E: E.tensor_scalar_add(BI(3), BI(0), 1.0),
                         [n_mx])
            n_md = g.add("vector", lambda E: E.tensor_add(BI(4), BI(2), BI(3)),
                         [n_lo, n_hi])
            n_md = g.add("vector", lambda E: E.tensor_scalar_mul(BI(4), BI(4), 0.5),
                         [n_md])
            last_cnt_reader = None
            last_mid_reader = None
            for it in range(BISECT_ITERS):
                n_bc = g.add("tensor", lambda E: E.matmul(
                    ps_cnt[:, 256:257], ones_row, BI(4), start=True, stop=True),
                    [n_md] + ([last_mid_reader] if last_mid_reader else []))
                n_mb = g.add("vector", lambda E: E.tensor_copy(mid_bc, ps_cnt[:, 256:257]),
                             [n_bc])
                last_mid_reader = n_mb
                n_gt = g.add("vector", lambda E: E.tensor_single_scalar(
                    gt, S(5), mid_bc, ALU.is_gt), [n_mb])
                n_ct = g.add("tensor", lambda E: E.matmul(
                    ps_cnt[0:1, 384:400], ones_col, gt, start=True, stop=True),
                    [n_gt] + ([last_cnt_reader] if last_cnt_reader else []))
                n_rd = g.add("vector", lambda E: E.reduce_sum(
                    BI(5), ps_cnt[0:1, 384:400], axis=AX.X), [n_ct])
                last_cnt_reader = n_rd
                n_pr = g.add("vector", lambda E: E.tensor_single_scalar(
                    BI(6), BI(5), float(K_CAP), ALU.is_ge), [n_rd])
                n_d1 = g.add("vector", lambda E: E.tensor_sub(BI(7), BI(4), BI(2)),
                             [n_pr])
                n_d2 = g.add("vector", lambda E: E.tensor_mul(BI(7), BI(7), BI(6)),
                             [n_d1])
                n_lo2 = g.add("vector", lambda E: E.tensor_add(BI(2), BI(2), BI(7)),
                              [n_d2])
                n_e1 = g.add("vector", lambda E: E.tensor_sub(BI(8), BI(3), BI(4)),
                             [n_pr])
                n_e2 = g.add("vector", lambda E: E.tensor_mul(BI(8), BI(8), BI(6)),
                             [n_e1])
                n_hi2 = g.add("vector", lambda E: E.tensor_add(BI(3), BI(4), BI(8)),
                              [n_e2, n_lo2])
                n_md = g.add("vector", lambda E: E.tensor_add(BI(4), BI(2), BI(3)),
                             [n_lo2, n_hi2])
                n_md = g.add("vector", lambda E: E.tensor_scalar_mul(
                    BI(4), BI(4), 0.5), [n_md])
            n_tb = g.add("tensor", lambda E: E.matmul(
                ps_cnt[:, 256:257], ones_row, BI(2), start=True, stop=True),
                [n_md, last_mid_reader])
            n_tau = g.add("vector", lambda E: E.tensor_copy(tau_bc, ps_cnt[:, 256:257]),
                          [n_tb])
            if dbg is not None:
                d1 = g.dma("sync", dbg["scores"][s, :, :], S(5), [n_s])
                d2 = g.dma("sync", dbg["tau"][s, :, :], tau_bc, [n_tau])
                d3 = g.dma("sync", dbg["minmax"][s, :, :], bis[0:1, 0:4], [n_tau])
                g.add("sync", lambda E: E.nop(), [d1, d2, d3])
            g.lower()

            g = Graph(nc, sems, counts)
            prev = {}
            for i in range(NTS):
                t = s * NTS + i
                b = i % 2
                xt_i = xt[:, i * 1024:(i + 1) * 1024]
                n_xn = g.add("vector", lambda E, xt_i=xt_i, b=b, i=i: E.tensor_scalar(
                    xn[b], xt_i, S(2)[:, i:i + 1], S(3)[:, i:i + 1],
                    ALU.subtract, ALU.mult),
                    [prev.get(("xn_rd", b)), n_last_load])
                n_cb = g.add("vector", lambda E, b=b: E.tensor_copy(xnbf[b], xn[b]),
                             [n_xn, prev.get(("xnbf_rd", b))])
                n_ev = transpose_tile_nodes(
                    g, ps_tr, xnbf[b], xf[b], ident_bf, [n_cb],
                    evac_eng="scalar", bank_free=prev.get("tr_bank"),
                    evac_deps=[prev.get(("xf_rd", b))])
                prev[("xnbf_rd", b)] = n_ev
                prev["tr_bank"] = n_ev
                mm_last = [None, None]
                for c in range(NC):
                    for h in range(2):
                        dep0 = [n_ev, prev.get("gelu_any"), n_wb]
                        mm_last[h] = g.add("tensor", lambda E, c=c, h=h, b=b: E.matmul(
                            ps_p[h], xf[b][:, c * 128:(c + 1) * 128],
                            wb_sb[:, c * 1024 + h * 512: c * 1024 + (h + 1) * 512],
                            start=(c == 0), stop=(c == NC - 1)),
                            dep0 if c == 0 else [])
                prev[("xf_rd", b)] = mm_last[1]
                n_g0 = g.add("scalar", lambda E, b=b: E.activation(
                    p_sb[b][:, 0:512], ps_p[0], AF.Gelu_apprx_tanh),
                    [mm_last[0], prev.get(("p_rd", b))])
                n_g1 = g.add("scalar", lambda E, b=b: E.activation(
                    p_sb[b][:, 512:1024], ps_p[1], AF.Gelu_apprx_tanh),
                    [mm_last[1]])
                prev[("gelu", b)] = n_g1
                prev["gelu_any"] = n_g1
                n_mc = g.add("vector", lambda E, i=i, b=b: E.tensor_single_scalar(
                    m_col[:, b:b + 1], S(5)[:, i:i + 1], tau_bc, ALU.is_gt),
                    [n_tau, prev.get(("mc_rd", b))])
                n_d = g.add("vector", lambda E, b=b: E.tensor_sub(
                    dd_sb[b], p_sb[b], xn[b]), [n_g0, n_g1, n_xn])
                prev[("p_rd", b)] = n_d
                n_dm = g.add("vector", lambda E, b=b: E.tensor_scalar_mul(
                    dd_sb[b], dd_sb[b], m_col[:, b:b + 1]), [n_d, n_mc])
                prev[("mc_rd", b)] = n_dm
                n_r1 = g.add("vector", lambda E, xt_i=xt_i, b=b: E.tensor_add(
                    out_sb[b], xt_i, xn[b]), [n_xn, prev.get(("out_st", b))])
                prev[("xn_rd", b)] = n_r1
                n_r2 = g.add("vector", lambda E, b=b: E.tensor_add(
                    out_sb[b], out_sb[b], dd_sb[b]), [n_r1, n_dm])
                n_st = g.dma("gpsimd", xout_t[:, t, :], out_sb[b], [n_r2])
                prev[("out_st", b)] = n_st
                if dbg is not None and s == 0 and i < 2:
                    dx = g.dma("sync", dbg["xn"][i, :, :], xn[b], [n_xn, n_d])
                    df = g.dma("sync", dbg["xf"][i, :, :], xf[b], [n_ev, mm_last[1]])
                    dp = g.dma("sync", dbg["p"][i, :, :], p_sb[b], [n_g1, n_d])
                    prev[("xn_rd", b)] = dx
                    prev[("xf_rd", b)] = df
                    prev[("p_rd", b)] = dp
            g.add("sync", lambda E: E.nop(), [prev[("out_st", 0)], prev[("out_st", 1)]])
            g.lower()


def emit_l0attn(nc, sems, counts, xin, xout, wq_ap, wk_ap, wv_ap, wo_ap):
    xin_t = xin.rearrange("(t p) d -> p t d", p=128)
    xout_t = xout.rearrange("(t p) d -> p t d", p=128)
    HW = 65
    with ExitStack() as ex:
        sb = lambda nm, sh, dt: ex.enter_context(nc.sbuf_tensor(nm, sh, dt))[:]
        psa = lambda nm, sh, dt=F32: ex.enter_context(nc.psum_tensor(nm, sh, dt))[:]

        ident_bf = sb("a0_id", [128, 128], BF16)
        ones_row = sb("a0_onr", [1, 128], F32)
        cpool = sb("a0_cp", [128, 3], F32)
        q_fm = sb("a0_q", [128, NC * T], BF16)
        k_fm = sb("a0_k", [128, NC * T], BF16)
        v_aug = sb("a0_v", [128, NTS * NHEAD * HW], BF16)
        att_qc = sb("a0_att", [128, NC * 512], BF16)
        buf_a = [sb(f"a0_ba{i}", [128, 1024], F32) for i in range(2)]
        xn_scr = sb("a0_xn", [128, 1024], F32)
        spt = sb("a0_sp", [128, 1024], F32)
        sqdump = sb("a0_sqd", [128, 1024], F32)
        bf_scr = [sb(f"a0_bf{i}", [128, 1024], BF16) for i in range(2)]
        bf_scr2 = [sb(f"a0_bg{i}", [128, 1024], BF16) for i in range(2)]
        vfm = [sb(f"a0_vf{i}", [128, 1024], BF16) for i in range(2)]
        pT = [sb(f"a0_pt{i}", [128, 512], BF16) for i in range(2)]
        rb_sb = sb("a0_rb", [128, 512], F32)
        rl_row = sb("a0_rl", [1, 1024], F32)
        sm = [sb(f"a0_sm{i}", [128, 24], F32) for i in range(2)]
        gst = sb("a0_gst", [128, 16 * 12], F32)
        hx = [sb(f"a0_hx{i}", [128, 32], F32) for i in range(2)]

        g = Graph(nc, sems, counts)
        emit_consts(g, dict(ident_bf=ident_bf, ones_row=ones_row))
        g.add("gpsimd", lambda E: E.memset(cpool[:, 0:1], EPS_LN))
        g.add("gpsimd", lambda E: E.memset(cpool[:, 1:2], 1.0 + EPS_HYP))
        g.add("gpsimd", lambda E: E.memset(cpool[:, 2:3], EPS_HYP))
        g.add("gpsimd", lambda E: E.memset(
            v_aug.rearrange("p (t h m) -> p t h m", h=NHEAD, m=HW)[:, :, :, 64:65],
            1.0))
        g.lower()

        def weight_graph(dst_list, ap_list, stage, ps_tr):
            g = Graph(nc, sems, counts)
            prev_ev = None
            prev_ld = {}
            for dst, ap in zip(dst_list, ap_list):
                dst_v = dst.rearrange("p (c o) -> p c o", c=NC)
                for rr in range(NC):
                    b = rr % 2
                    n_l = g.dma("gpsimd", stage[b], ap[rr * 128:(rr + 1) * 128, :],
                                [prev_ld.get(b)])
                    n_ev = transpose_tile_nodes(
                        g, ps_tr, stage[b], dst_v[:, :, rr * 128:(rr + 1) * 128],
                        ident_bf, [n_l], evac_eng="scalar", bank_free=prev_ev)
                    prev_ld[b] = n_ev
                    prev_ev = n_ev
            g.last_ev = prev_ev
            g.lower()
            return g

        CH1 = cpool[:, 1:2]
        CH2 = cpool[:, 2:3]

        for s in range(SEQ_PER_CORE):
            with ExitStack() as exw:
                sbw = lambda nm, sh, dt: exw.enter_context(
                    nc.sbuf_tensor(nm, sh, dt, side="right"))[:]
                stage = [sbw(f"a0_st{s}{i}", [128, 1024], BF16) for i in range(2)]
                wqT = sbw(f"a0_wq{s}", [128, NC * 1024], BF16)
                wkT = sbw(f"a0_wk{s}", [128, NC * 1024], BF16)
                wvT = sbw(f"a0_wv{s}", [128, NC * 1024], BF16)
                ps_tr = exw.enter_context(
                    nc.psum_tensor(f"a0_ptr{s}", [128, 1024], BF16))[:]
                ps_trB = exw.enter_context(
                    nc.psum_tensor(f"a0_ptrB{s}", [128, 1024], BF16))[:]
                psb = [exw.enter_context(
                    nc.psum_tensor(f"a0_p{s}{i}", [128, 512], F32))[:]
                    for i in range(6)]
                n_wld = weight_graph([wqT, wkT, wvT], [wq_ap, wk_ap, wv_ap],
                                     stage, ps_tr).last_ev

                GS = lambda k: gst[:, 16 * k:16 * (k + 1)]
                g = Graph(nc, sems, counts)
                prev = {}
                for i in range(NTS):
                    t = s * NTS + i
                    b = i % 2
                    n_l = g.dma("sync", buf_a[b], xin_t[:, t, :],
                                [prev.get(("xt_rd", b)), prev.get(("xt_rd2", b))])
                    n_s1 = g.add("vector", lambda E, b=b, i=i: E.reduce_sum(
                        GS(0)[:, i:i + 1], buf_a[b], axis=AX.X), [n_l])
                    n_sq_ = g.add("scalar", lambda E, b=b: E.activation(
                        sqdump, buf_a[b], AF.Square),
                        [n_l, prev.get("sq_war")])
                    n_s2 = g.add("vector", lambda E, i=i: E.reduce_sum(
                        GS(1)[:, i:i + 1], sqdump, axis=AX.X), [n_sq_])
                    prev["sq_war"] = n_s2
                    prev[("xt_rd", b)] = n_s1
                    prev[("xt_rd2", b)] = n_sq_
                n_mu = g.add("vector", lambda E: E.tensor_scalar_mul(
                    GS(2), GS(0), 1.0 / D), [n_s2])
                n_e2_ = g.add("vector", lambda E: E.tensor_scalar_mul(
                    GS(7), GS(1), 1.0 / D), [n_s2])
                n_m2 = g.add("vector", lambda E: E.tensor_mul(GS(8), GS(2), GS(2)),
                             [n_mu])
                n_vr = g.add("vector", lambda E: E.tensor_sub(GS(7), GS(7), GS(8)),
                             [n_e2_, n_m2])
                n_ln = g.add("scalar", lambda E: E.activation(
                    GS(8), GS(7), AF.Ln, bias=cpool[:, 0:1]), [n_vr, n_sq_])
                n_rs = g.add("scalar", lambda E: E.activation(
                    GS(3), GS(8), AF.Exp, scale=-0.5), [n_ln])
                for i in range(NTS):
                    t = s * NTS + i
                    b = i % 2
                    n_l2 = g.dma("sync", buf_a[b], xin_t[:, t, :],
                                 [prev.get(("xt_rd", b)), prev.get(("xt_rd2", b))])
                    n_xn = g.add("vector", lambda E, b=b, i=i: E.tensor_scalar(
                        xn_scr, buf_a[b], GS(2)[:, i:i + 1], GS(3)[:, i:i + 1],
                        ALU.subtract, ALU.mult),
                        [n_l2, n_rs, prev.get("xn_rd")])
                    prev[("xt_rd", b)] = n_xn
                    prev[("xt_rd2", b)] = n_xn
                    n_sp = g.add("vector", lambda E: E.tensor_scalar(
                        spt[:, 1:1024], xn_scr[:, 1:1024], 8.0, -8.0,
                        ALU.min, ALU.max), [n_xn, prev.get("sp_rd")])
                    prev["xn_rd"] = n_sp
                    n_sq2_ = g.add("scalar", lambda E: E.activation(
                        sqdump[:, 1:1024], spt[:, 1:1024], AF.Square),
                        [n_sp, prev.get("sq_war")])
                    prev["sp_rd"] = n_sq2_
                    n_ssp = g.add("vector", lambda E, i=i: E.reduce_sum(
                        GS(4)[:, i:i + 1], sqdump[:, 1:1024], axis=AX.X), [n_sq2_])
                    prev["sq_war"] = n_ssp
                n = g.add("scalar", lambda E: E.activation(
                    GS(7), GS(4), AF.Ln, bias=CH1), [n_ssp, n_sq2_])
                n = g.add("scalar", lambda E: E.activation(
                    GS(8), GS(7), AF.Exp, scale=0.5), [n])
                n_z = g.add("vector", lambda E: E.tensor_single_scalar(
                    GS(9), GS(8), CH1, ALU.max), [n])
                n = g.add("scalar", lambda E: E.activation(
                    GS(7), GS(9), AF.Square), [n_z])
                n = g.add("vector", lambda E: E.tensor_scalar_add(
                    GS(7), GS(7), -1.0), [n])
                n = g.add("scalar", lambda E: E.activation(
                    GS(10), GS(7), AF.Ln), [n])
                n = g.add("scalar", lambda E: E.activation(
                    GS(10), GS(10), AF.Exp, scale=0.5), [n])
                n = g.add("vector", lambda E: E.tensor_add(
                    GS(10), GS(9), GS(10)), [n])
                n = g.add("scalar", lambda E: E.activation(
                    GS(10), GS(10), AF.Ln), [n])
                n_d = g.add("vector", lambda E: E.tensor_single_scalar(
                    GS(10), GS(10), 0.001, ALU.max), [n])
                n_d0 = g.add("vector", lambda E: E.tensor_sub(
                    GS(11), GS(8), GS(9)), [n_z, n_d])
                n = g.add("scalar", lambda E: E.activation(
                    GS(7), GS(11), AF.Square), [n_d0])
                n = g.add("vector", lambda E: E.tensor_sub(GS(7), GS(4), GS(7)),
                          [n])
                n = g.add("vector", lambda E: E.tensor_single_scalar(
                    GS(7), GS(7), CH2, ALU.max), [n])
                n = g.add("scalar", lambda E: E.activation(
                    GS(7), GS(7), AF.Ln), [n])
                n = g.add("scalar", lambda E: E.activation(
                    GS(7), GS(7), AF.Exp, scale=0.5), [n])
                n = g.add("vector", lambda E: E.tensor_single_scalar(
                    GS(7), GS(7), CH2, ALU.add), [n])
                n = g.add("vector", lambda E: E.reciprocal(GS(7), GS(7)), [n])
                n_scl = g.add("vector", lambda E: E.tensor_mul(
                    GS(5), GS(10), GS(7)), [n, n_d])
                n_v0 = g.add("vector", lambda E: E.tensor_mul(
                    GS(6), GS(11), GS(5)), [n_scl])
                def emit_qk_tr(pi, pb, pcq, pck):
                    n_evq = transpose_tile_nodes(
                        g, ps_trB, bf_scr[pb],
                        q_fm.rearrange("p (c t2) -> p c t2",
                                       c=NC)[:, :, pi * 128:(pi + 1) * 128],
                        ident_bf, [pcq], evac_eng="scalar",
                        bank_free=prev.get("tr_bankB"))
                    prev[("bf_rd", pb)] = n_evq
                    n_evk = transpose_tile_nodes(
                        g, ps_trB, bf_scr2[pb],
                        k_fm.rearrange("p (c t2) -> p c t2",
                                       c=NC)[:, :, pi * 128:(pi + 1) * 128],
                        ident_bf, [pck], evac_eng="scalar",
                        bank_free=n_evq)
                    prev[("bf2_rd", pb)] = n_evk
                    prev["tr_bankB"] = n_evk

                pend_tr = None
                for i in range(NTS):
                    t = s * NTS + i
                    b = i % 2
                    n_l3 = g.dma("sync", buf_a[b], xin_t[:, t, :],
                                 [prev.get(("xt_rd", b)), prev.get(("xt_rd2", b))])
                    n_xn3 = g.add("vector", lambda E, b=b, i=i: E.tensor_scalar(
                        xn_scr, buf_a[b], GS(2)[:, i:i + 1], GS(3)[:, i:i + 1],
                        ALU.subtract, ALU.mult),
                        [n_l3, n_v0, prev.get("xn_rd")])
                    prev[("xt_rd", b)] = n_xn3
                    prev[("xt_rd2", b)] = n_xn3
                    n_sp3 = g.add("vector", lambda E: E.tensor_scalar(
                        spt[:, 1:1024], xn_scr[:, 1:1024], 8.0, -8.0,
                        ALU.min, ALU.max), [n_xn3, prev.get("sp_rd")])
                    prev["xn_rd"] = n_sp3
                    n_vb = g.add("vector", lambda E, b=b, i=i: E.tensor_scalar_mul(
                        bf_scr[b][:, 1:1024], spt[:, 1:1024], GS(5)[:, i:i + 1]),
                        [n_sp3, prev.get(("bf_rd", b))])
                    prev["sp_rd"] = n_vb
                    n_vc = g.add("vector", lambda E, b=b, i=i: E.tensor_copy(
                        bf_scr[b][:, 0:1], GS(6)[:, i:i + 1]), [n_vb])
                    n_ev = transpose_tile_nodes(
                        g, ps_tr, bf_scr[b], vfm[b], ident_bf, [n_vb, n_vc],
                        evac_eng="scalar", bank_free=prev.get("tr_bankA"),
                        evac_deps=[prev.get(("vfm_rd", b))])
                    prev[("bf_rd", b)] = n_ev
                    prev["tr_bankA"] = n_ev
                    mm = [None] * 6
                    for c in range(NC):
                        for wi, wT in enumerate((wqT, wkT, wvT)):
                            for h in range(2):
                                k6 = wi * 2 + h
                                dep0 = []
                                if c == 0:
                                    dep0 = [n_ev, prev.get(("bank_rd", k6)), n_wld]
                                mm[k6] = g.add("tensor",
                                               lambda E, c=c, wT=wT, h=h, b=b, k6=k6:
                                               E.matmul(
                                    psb[k6], vfm[b][:, c * 128:(c + 1) * 128],
                                    wT[:, c * 1024 + h * 512: c * 1024 + (h + 1) * 512],
                                    start=(c == 0), stop=(c == NC - 1)), dep0)
                    prev[("vfm_rd", b)] = mm[5]

                    if pend_tr is not None:
                        emit_qk_tr(*pend_tr)

                    cast_qk = [None, None]
                    for which, (b0, b1) in enumerate(((0, 1), (2, 3))):
                        n_q0 = g.add("vector", lambda E, b0=b0: E.tensor_scalar(
                            xn_scr[:, 0:512], psb[b0], 5.0, -5.0, ALU.min, ALU.max),
                            [mm[b0], prev.get("xn_rd")])
                        n_q1 = g.add("vector", lambda E, b1=b1: E.tensor_scalar(
                            xn_scr[:, 512:1024], psb[b1], 5.0, -5.0, ALU.min, ALU.max),
                            [mm[b1]])
                        prev[("bank_rd", b0)] = n_q0
                        prev[("bank_rd", b1)] = n_q1
                        n_qs = g.add("scalar", lambda E: E.activation(
                            sqdump, xn_scr, AF.Square), [n_q1, prev["sq_war"]])
                        prev["sq_war"] = n_qs
                        n_hs = g.add("vector", lambda E, b=b: E.reduce_sum(
                            hx[b][:, 0:16], sqdump.rearrange("p (h k) -> p h k", h=16),
                            axis=AX.X), [n_qs, prev.get(("hx_rd", b))])
                        n_cq = g.add("vector", lambda E, b=b: E.tensor_tensor(
                            sqdump[:, 0:16], xn_scr[:, 0:1024:64],
                            xn_scr[:, 0:1024:64], ALU.mult),
                            [n_q0, n_q1, n_qs, n_hs])
                        n_sub = g.add("vector", lambda E, b=b: E.tensor_sub(
                            hx[b][:, 16:32], hx[b][:, 0:16], sqdump[:, 0:16]),
                            [n_hs, n_cq])
                        n_x1 = g.add("scalar", lambda E, b=b: E.activation(
                            hx[b][:, 0:16], hx[b][:, 16:32], AF.Ln, bias=CH1), [n_sub])
                        n_x2 = g.add("scalar", lambda E, b=b: E.activation(
                            hx[b][:, 16:32], hx[b][:, 0:16], AF.Exp, scale=0.5), [n_x1])
                        sgn = -1.0 if which == 0 else 1.0
                        n_wr = g.add("vector", lambda E, b=b, sgn=sgn:
                                     E.tensor_scalar_mul(
                            xn_scr[:, 0:1024:64], hx[b][:, 16:32], sgn), [n_x2])
                        prev[("hx_rd", b)] = n_wr
                        if which == 0:
                            n_cast = g.add("vector", lambda E, b=b: E.tensor_copy(
                                bf_scr[b], xn_scr),
                                [n_wr, prev.get(("bf_rd", b))])
                        else:
                            n_cast = g.add("vector", lambda E, b=b: E.tensor_copy(
                                bf_scr2[b], xn_scr),
                                [n_wr, prev.get(("bf2_rd", b))])
                        prev["xn_rd"] = n_cast
                        cast_qk[which] = n_cast
                    pend_tr = (i, b, cast_qk[0], cast_qk[1])
                    va = v_aug[:, t0v(i):t0v(i) + NHEAD * HW].rearrange(
                        "p (h m) -> p h m", m=HW)
                    n_v1 = g.add("vector", lambda E, va=va: E.tensor_copy(
                        va[:, 0:8, 0:64],
                        psb[4].rearrange("p (h k) -> p h k", h=8)), [mm[4]])
                    n_v2 = g.add("vector", lambda E, va=va: E.tensor_copy(
                        va[:, 8:16, 0:64],
                        psb[5].rearrange("p (h k) -> p h k", h=8)), [mm[5]])
                    prev[("bank_rd", 4)] = n_v1
                    prev[("bank_rd", 5)] = n_v2
                emit_qk_tr(*pend_tr)
                g.lower()
                exw.pop_all().close()

            with ExitStack() as exw:
                woT = exw.enter_context(
                    nc.sbuf_tensor(f"a0_wo{s}", [128, NC * 1024], BF16,
                                   side="right"))[:]
                with ExitStack() as exw2:
                    sbw2 = lambda nm, sh, dt: exw2.enter_context(
                        nc.sbuf_tensor(nm, sh, dt, side="right"))[:]
                    stage = [sbw2(f"a0_so{s}{i}", [128, 1024], BF16)
                             for i in range(2)]
                    ps_wg = exw2.enter_context(
                        nc.psum_tensor(f"a0_pwg{s}", [128, 1024], BF16))[:]
                    weight_graph([woT], [wo_ap], stage, ps_wg)
                ps_s2 = [exw.enter_context(
                    nc.psum_tensor(f"a0_ps2{s}{i}", [128, 1024], F32))[:]
                    for i in range(2)]
                ps_pv = exw.enter_context(
                    nc.psum_tensor(f"a0_ppv{s}", [128, 1024], F32))[:]
                ps_wo = exw.enter_context(
                    nc.psum_tensor(f"a0_pwo{s}", [128, 1024], F32))[:]
                pT2 = [exw.enter_context(
                    nc.sbuf_tensor(f"a0_pT2{s}{i}", [128, 1024], BF16,
                                   side="right"))[:] for i in range(4)]
                att2 = [exw.enter_context(
                    nc.sbuf_tensor(f"a0_at2{s}{i}", [128, NC * 512], BF16,
                                   side="right"))[:] for i in range(2)]

                g = Graph(nc, sems, counts)
                prev = {}
                wo_queue = []

                def emit_pair(hp, qc, pending_norm):
                    att_c = att2[qc % 2]
                    cT = hp * T
                    pvs = [None, None]
                    exps = {}
                    for kt in range(NTS):
                        pb = kt % 2
                        pb4 = kt % 4
                        sb_ = ps_s2[pb]
                        n_mmA = g.add("tensor", lambda E, cT=cT, kt=kt, qc=qc,
                                      sb_=sb_: E.matmul(
                            sb_[:, 0:512],
                            k_fm[0:64, cT + kt * 128:cT + (kt + 1) * 128],
                            q_fm[0:64, cT + qc * 512:cT + (qc + 1) * 512],
                            start=True, stop=True, tile_position=(0, 0)),
                            [prev.get(("exp", pb))])
                        n_mmB = g.add("tensor", lambda E, cT=cT, kt=kt, qc=qc,
                                      sb_=sb_: E.matmul(
                            sb_[:, 512:1024],
                            k_fm[64:128, cT + kt * 128:cT + (kt + 1) * 128],
                            q_fm[64:128, cT + qc * 512:cT + (qc + 1) * 512],
                            start=True, stop=True, tile_position=(64, 0)), [])
                        n_ex = g.add("scalar", lambda E, pb4=pb4, sb_=sb_:
                                     E.activation(pT2[pb4], sb_, AF.Exp,
                                                  scale=0.125),
                            [n_mmB, prev.get(("pT_rd", pb4))])
                        prev[("exp", pb)] = n_ex
                        exps[kt] = n_ex
                        if kt == 2 and pending_norm is not None:
                            emit_norm(*pending_norm)
                            pending_norm = None
                        if kt >= 3:
                            kp = kt - 3
                            pvs = emit_pv(hp, kp, exps[kp], pvs)
                    for kp in (NTS - 3, NTS - 2, NTS - 1):
                        pvs = emit_pv(hp, kp, exps[kp], pvs)
                    return (hp, qc, pvs)

                def emit_norm(hp, qc, pvs):
                    att_c = att2[qc % 2]
                    n_rlA = g.add("vector", lambda E: E.reciprocal(
                        rl_row[:, 0:512], ps_pv[64:65, 0:512]),
                        [pvs[0], prev.get("rbc_done")])
                    n_rlB = g.add("vector", lambda E: E.reciprocal(
                        rl_row[:, 512:1024], ps_pv[64:65, 512:1024]),
                        [pvs[1]])
                    n_rbcA = g.add("tensor", lambda E: E.matmul(
                        ps_pv[64:128, 0:512], ones_row[:, 0:64],
                        rl_row[:, 0:512], start=True, stop=True), [n_rlA])
                    n_rbcB = g.add("tensor", lambda E: E.matmul(
                        ps_pv[64:128, 512:1024], ones_row[:, 0:64],
                        rl_row[:, 512:1024], start=True, stop=True), [n_rlB])
                    prev["rbc_done"] = n_rbcB
                    n_rbeA = g.add("vector", lambda E: E.tensor_copy(
                        rb_sb[0:64, 0:512], ps_pv[64:128, 0:512]), [n_rbcA])
                    n_rbeB = g.add("vector", lambda E: E.tensor_copy(
                        rb_sb[64:128, 0:512], ps_pv[64:128, 512:1024]), [n_rbcB])
                    n_attA = g.add("vector", lambda E, att_c=att_c, hp=hp:
                                   E.tensor_mul(
                        att_c[0:64, hp * 512:(hp + 1) * 512],
                        ps_pv[0:64, 0:512], rb_sb[0:64, 0:512]),
                        [n_rbeA, prev.get(("att_rd", qc % 2))])
                    n_attB = g.add("vector", lambda E, att_c=att_c, hp=hp:
                                   E.tensor_mul(
                        att_c[64:128, hp * 512:(hp + 1) * 512],
                        ps_pv[0:64, 512:1024], rb_sb[64:128, 0:512]),
                        [n_rbeB])
                    prev["pv_free"] = (n_attA, n_attB)
                    return n_attB

                def emit_pv(hp, kt, n_ex, pvs):
                    for hb in range(2):
                        dd = [n_ex]
                        if kt == 0:
                            dd += list(prev.get("pv_free", ()))
                        pvs[hb] = g.add("tensor", lambda E, hp=hp, kt=kt, hb=hb:
                                        E.matmul(
                            ps_pv[0:HW, hb * 512:(hb + 1) * 512],
                            v_aug[:, t0v(kt) + (hp * 2 + hb) * HW:
                                  t0v(kt) + (hp * 2 + hb) * HW + HW],
                            pT2[kt % 4][:, hb * 512:(hb + 1) * 512],
                            start=(kt == 0), stop=(kt == NTS - 1)), dd)
                    prev[("pT_rd", kt % 4)] = pvs[1]
                    return pvs

                def emit_wo_tile(qc, tt):
                    t = s * NTS + qc * 4 + tt
                    b = tt % 2
                    att_c = att2[qc % 2]
                    mmw = [None, None]
                    for c in range(NC):
                        for h2 in range(2):
                            dep0 = []
                            if c == 0:
                                dep0 = [prev.get(("att_done", qc % 2)),
                                        prev.get(("oev", b)),
                                        prev.get("wo_bank")]
                            mmw[h2] = g.add("tensor", lambda E, c=c, h2=h2, tt=tt,
                                            att_c=att_c: E.matmul(
                                ps_wo[:, h2 * 512:(h2 + 1) * 512],
                                att_c[:, c * 512 + tt * 128:c * 512 + (tt + 1) * 128],
                                woT[:, c * 1024 + h2 * 512:c * 1024 + (h2 + 1) * 512],
                                start=(c == 0), stop=(c == NC - 1)), dep0)
                    prev[("att_rd", qc % 2)] = mmw[1]
                    n_oe = g.add("scalar", lambda E, b=b: E.activation(
                        buf_a[b][:, 0:512], ps_wo[:, 0:512], AF.Copy),
                        [mmw[0], prev.get(("oa_rd", b))])
                    n_oe2 = g.add("scalar", lambda E, b=b: E.activation(
                        buf_a[b][:, 512:1024], ps_wo[:, 512:1024], AF.Copy),
                        [mmw[1], prev.get(("oa_rd", b))])
                    prev["wo_bank"] = n_oe2
                    n_vs_ = g.add("scalar", lambda E, b=b: E.activation(
                        sqdump[:, 1:1024], buf_a[b][:, 1:1024], AF.Square),
                        [n_oe, n_oe2, prev.get("sq_war"), prev.get(("sm_rd", b))])
                    n_vs = g.add("vector", lambda E, b=b: E.reduce_sum(
                        sm[b][:, 0:1], sqdump[:, 1:1024], axis=AX.X), [n_vs_])
                    prev["sq_war"] = n_vs
                    n_v1 = g.add("scalar", lambda E, b=b: E.activation(
                        sm[b][:, 1:2], sm[b][:, 0:1], AF.Ln, bias=CH2), [n_vs])
                    n_v2 = g.add("scalar", lambda E, b=b: E.activation(
                        sm[b][:, 2:3], sm[b][:, 1:2], AF.Exp, scale=0.5), [n_v1])
                    n_vn = g.add("vector", lambda E, b=b: E.tensor_single_scalar(
                        sm[b][:, 3:4], sm[b][:, 2:3], 5.0, ALU.min), [n_v2])
                    n_e1 = g.add("scalar", lambda E, b=b: E.activation(
                        sm[b][:, 4:5], sm[b][:, 3:4], AF.Exp), [n_vn])
                    n_e2 = g.add("scalar", lambda E, b=b: E.activation(
                        sm[b][:, 5:6], sm[b][:, 3:4], AF.Exp, scale=-1.0), [n_e1])
                    n_sh = g.add("vector", lambda E, b=b: E.tensor_sub(
                        sm[b][:, 6:7], sm[b][:, 4:5], sm[b][:, 5:6]), [n_e2])
                    n_de = g.add("vector", lambda E, b=b: E.tensor_single_scalar(
                        sm[b][:, 7:8], sm[b][:, 3:4], CH2, ALU.add), [n_vn])
                    n_rc = g.add("vector", lambda E, b=b: E.reciprocal(
                        sm[b][:, 8:9], sm[b][:, 7:8]), [n_de])
                    n_sc = g.add("vector", lambda E, b=b: E.tensor_scalar(
                        sm[b][:, 9:10], sm[b][:, 6:7], sm[b][:, 8:9], 0.5,
                        ALU.mult, ALU.mult), [n_sh, n_rc])
                    n_s1 = g.add("vector", lambda E, b=b: E.tensor_scalar(
                        spt[:, 1:1024], buf_a[b][:, 1:1024], sm[b][:, 9:10], 8.0,
                        ALU.mult, ALU.min), [n_sc, prev.get("sp_rd")])
                    n_s2 = g.add("vector", lambda E: E.tensor_scalar_max(
                        spt[:, 1:1024], spt[:, 1:1024], -8.0), [n_s1])
                    n_q2_ = g.add("scalar", lambda E, b=b: E.activation(
                        sqdump[:, 1:1024], spt[:, 1:1024], AF.Square),
                        [n_s2, prev["sq_war"]])
                    n_q2 = g.add("vector", lambda E, b=b: E.reduce_sum(
                        sm[b][:, 10:11], sqdump[:, 1:1024], axis=AX.X), [n_q2_])
                    prev["sq_war"] = n_q2
                    n_x1 = g.add("scalar", lambda E, b=b: E.activation(
                        sm[b][:, 11:12], sm[b][:, 10:11], AF.Ln, bias=CH1), [n_q2])
                    n_x2 = g.add("scalar", lambda E, b=b: E.activation(
                        sm[b][:, 12:13], sm[b][:, 11:12], AF.Exp, scale=0.5),
                        [n_x1])
                    n_lx = g.dma("scalar", xn_scr, xin_t[:, t, :],
                                 [prev.get("xn_rd2")])
                    n_r1 = g.add("vector", lambda E, b=b: E.tensor_add(
                        buf_a[b][:, 1:1024], xn_scr[:, 1:1024], spt[:, 1:1024]),
                        [n_s2, n_lx, n_oe2, n_vs])
                    prev["sp_rd"] = n_r1
                    n_r0 = g.add("vector", lambda E, b=b: E.tensor_add(
                        buf_a[b][:, 0:1], xn_scr[:, 0:1], sm[b][:, 12:13]),
                        [n_x2, n_lx])
                    prev[("sm_rd", b)] = n_r0
                    prev["xn_rd2"] = n_r0
                    n_st = g.dma("gpsimd", xout_t[:, t, :], buf_a[b],
                                 [n_r1, n_r0])
                    prev[("oa_rd", b)] = n_st
                    prev[("oev", b)] = n_st

                import os as _os
                _skip_wo = _os.environ.get("A0_SKIP_WO") == "1"
                _no_ilv = _os.environ.get("A0_ILV") != "1"
                pending_norm = None
                for qc in range(4):
                    for hp in range(NC):
                        pending_norm = emit_pair(hp, qc, pending_norm)
                        if not _skip_wo and not _no_ilv and wo_queue and hp >= 4:
                            emit_wo_tile(*wo_queue.pop(0))
                    att_done = emit_norm(*pending_norm)
                    pending_norm = None
                    prev[("att_done", qc % 2)] = att_done
                    if _skip_wo:
                        prev[("att_rd", qc % 2)] = att_done
                        prev[("oa_rd", qc % 4 % 2)] = g.dma(
                            "sync", xout_t[:, s * NTS + qc * 4, :],
                            buf_a[0], [att_done])
                    else:
                        wo_queue.extend((qc, tt) for tt in range(4))
                while wo_queue:
                    emit_wo_tile(*wo_queue.pop(0))
                g.add("sync", lambda E: E.nop(),
                      [prev[("oa_rd", 0)], prev[("oa_rd", 1)]])
                g.lower()


def t0v(i):
    return i * NHEAD * 65


def emit_l1attn(nc, sems, counts, xin, xout, wqkv_ap, dbg=None):
    xin_t = xin.rearrange("(t p) d -> p t d", p=128)
    xout_t = xout.rearrange("(t p) d -> p t d", p=128)
    SCALE = 1.0 / (D ** 0.5)
    with ExitStack() as ex:
        sb = lambda nm, sh, dt: ex.enter_context(nc.sbuf_tensor(nm, sh, dt))[:]
        psa = lambda nm, sh, dt=F32: ex.enter_context(nc.psum_tensor(nm, sh, dt))[:]

        wqkv = sb("a1_w", [128, NC * 3072], BF16)
        xn_fm = sb("a1_xf", [128, NC * T], BF16)
        q_fm = sb("a1_q", [128, NC * T], BF16)
        k_fm = sb("a1_k", [128, NC * T], BF16)
        v_tm = sb("a1_v", [128, NTS * 1024], BF16)
        ident_bf = sb("a1_id", [128, 128], BF16)
        tri = sb("a1_tri", [128, 128], F32)
        cpool = sb("a1_cp", [128, 1], F32)
        stats = sb("a1_st", [128, 16 * 6], F32)
        S = lambda k: stats[:, 16 * k:16 * (k + 1)]
        sqdump = sb("a1_sqd", [128, 1024], F32)
        xt = [sb(f"a1_xt{i}", [128, 1024], F32) for i in range(2)]
        xnbf = [sb(f"a1_xnb{i}", [128, 1024], BF16) for i in range(2)]
        p_sb = sb("a1_p", [128, 2048], BF16)
        pt_sb = [sb(f"a1_pt{i}", [128, 128], BF16) for i in range(2)]
        lrow = sb("a1_l", [128, 8], F32)
        o_sb = [sb(f"a1_o{i}", [128, 1024], F32) for i in range(2)]
        ps_tr = psa("a1_ptr", [128, 1024], BF16)
        ps_s = psa("a1_ps", [128, 2048])
        ps_pt = psa("a1_ppt", [128, 128], BF16)
        ps_o = psa("a1_po", [128, 1024])

        g = Graph(nc, sems, counts)
        emit_consts(g, dict(ident_bf=ident_bf))
        n0 = g.add("gpsimd", lambda E: E.memset(tri, 0.0))
        g.add("gpsimd", lambda E: E.affine_select(
            out=tri, in_=tri, compare_op=ALU.is_ge, fill=-1e30, base=0,
            pattern=[[-1, 128]], channel_multiplier=1), deps=[n0])
        g.add("gpsimd", lambda E: E.memset(cpool[:, 0:1], EPS_LN))
        n_wqkv = g.dma("gpsimd", wqkv.rearrange("p (c e) -> p c e", c=NC),
                       wqkv_ap.rearrange("(c p) e -> p c e", p=128))
        g.lower(drain_gpsimd=False)

        for s in range(SEQ_PER_CORE):
            g = Graph(nc, sems, counts)
            n_prev_sq = None
            prev_ld = {}
            for i in range(NTS):
                t = s * NTS + i
                b = i % 2
                n_l = g.dma("sync", xt[b], xin_t[:, t, :], [prev_ld.get(b)])
                _, n_prev_sq = ln_stats_nodes(
                    g, xt[b], S(0)[:, i:i + 1], S(1)[:, i:i + 1], sqdump,
                    [n_l], sq_war=n_prev_sq)
                prev_ld[b] = n_prev_sq
            n_mu, n_rstd = ln_chain_nodes(
                g, S(0), S(1), S(2), S(3), S(4), S(5),
                [n_prev_sq, prev_ld[0], prev_ld[1]], cpool[:, 0:1])
            g.lower()

            g = Graph(nc, sems, counts)
            prev = {}
            for i in range(NTS):
                t = s * NTS + i
                b = i % 2
                n_l = g.dma("sync", xt[b], xin_t[:, t, :], [prev.get(("xt_rd", b))])
                n_xn = g.add("vector", lambda E, b=b, i=i: E.tensor_scalar(
                    xnbf[b], xt[b], S(2)[:, i:i + 1], S(3)[:, i:i + 1],
                    ALU.subtract, ALU.mult), [n_l, prev.get(("xnbf_rd", b))])
                prev[("xt_rd", b)] = n_xn
                n_ev = transpose_tile_nodes(
                    g, ps_tr, xnbf[b],
                    xn_fm.rearrange("p (c t2) -> p c t2",
                                    c=NC)[:, :, i * 128:(i + 1) * 128],
                    ident_bf, [n_xn], evac_eng="scalar",
                    bank_free=prev.get("tr_bank"))
                prev[("xnbf_rd", b)] = n_ev
                prev["tr_bank"] = n_ev
            n_xnfm = n_ev
            last_evac = None
            for which, dst in ((0, q_fm), (1, k_fm)):
                for o in range(NC):
                    mm = None
                    for c in range(NC):
                        for tc in range(4):
                            dep0 = []
                            if c == 0 and tc == 0:
                                dep0 = [n_xnfm, last_evac, n_wqkv]
                            mm = g.add("tensor", lambda E, which=which, o=o, c=c, tc=tc:
                                       E.matmul(
                                ps_s[:, tc * 512:(tc + 1) * 512],
                                wqkv[:, c * 3072 + which * 1024 + o * 128:
                                     c * 3072 + which * 1024 + (o + 1) * 128],
                                xn_fm[:, c * T + tc * 512: c * T + (tc + 1) * 512],
                                start=(c == 0), stop=(c == NC - 1)), dep0)
                    last_evac = g.add("scalar", lambda E, dst=dst, o=o: E.copy(
                        dst[:, o * T:(o + 1) * T], ps_s[:, 0:2048]), [mm])
            for i in range(NTS):
                mm2 = [None, None]
                for c in range(NC):
                    for h in range(2):
                        dep0 = []
                        if c == 0 and h == 0:
                            dep0 = [n_xnfm, prev.get("v_evac") or last_evac, n_wqkv]
                        mm2[h] = g.add("tensor", lambda E, i=i, c=c, h=h: E.matmul(
                            ps_o[:, h * 512:(h + 1) * 512],
                            xn_fm[:, c * T + i * 128: c * T + (i + 1) * 128],
                            wqkv[:, c * 3072 + 2048 + h * 512:
                                 c * 3072 + 2048 + (h + 1) * 512],
                            start=(c == 0), stop=(c == NC - 1)), dep0)
                prev["v_evac"] = g.add("scalar", lambda E, i=i: E.copy(
                    v_tm[:, i * 1024:(i + 1) * 1024], ps_o[:, 0:1024]),
                    [mm2[0], mm2[1]])
            g.lower()

            g = Graph(nc, sems, counts)
            prev = {}
            for i in range(NTS):
                t = s * NTS + i
                b = i % 2
                kspan = (i + 1) * 128
                nk5 = (kspan + 511) // 512
                mm = None
                for c in range(NC):
                    for kc in range(nk5):
                        w = min(512, kspan - kc * 512)
                        dep0 = []
                        if c == 0 and kc == 0:
                            dep0 = [prev.get("exp_done"), prev.get("mask_done")]
                        mm = g.add("tensor", lambda E, i=i, c=c, kc=kc, w=w: E.matmul(
                            ps_s[:, kc * 512: kc * 512 + w],
                            q_fm[:, c * T + i * 128:c * T + (i + 1) * 128],
                            k_fm[:, c * T + kc * 512: c * T + kc * 512 + w],
                            start=(c == 0), stop=(c == NC - 1)), dep0)
                n_mask = g.add("vector", lambda E, i=i: E.tensor_add(
                    ps_s[:, i * 128:(i + 1) * 128],
                    ps_s[:, i * 128:(i + 1) * 128], tri), [mm])
                prev["mask_done"] = n_mask
                n_mx = g.add("vector", lambda E, kspan=kspan: E.reduce_max(
                    lrow[:, 4:5], ps_s[:, 0:kspan], axis=AX.X),
                    [n_mask, prev.get("exp_done")])
                n_mg = g.add("vector", lambda E: E.tensor_scalar_mul(
                    lrow[:, 5:6], lrow[:, 4:5], -SCALE), [n_mx])
                n_exp = None
                for kc in range(nk5):
                    w = min(512, kspan - kc * 512)
                    n_exp = g.add("scalar", lambda E, kc=kc, w=w: E.activation(
                        p_sb[:, kc * 512:kc * 512 + w], ps_s[:, kc * 512:kc * 512 + w],
                        AF.Exp, bias=lrow[:, 5:6], scale=SCALE),
                        [n_mg] + ([prev.get("pv_done"), prev.get("l_done")]
                                  if kc == 0 else []))
                prev["exp_done"] = n_exp
                n_l = g.add("vector", lambda E, kspan=kspan: E.reduce_sum(
                    lrow[:, 6:7], p_sb[:, 0:kspan], axis=AX.X), [n_exp])
                n_rl = g.add("vector", lambda E: E.reciprocal(
                    lrow[:, 7:8], lrow[:, 6:7]), [n_l])
                prev["l_done"] = n_rl
                pv_last = None
                for kt in range(i + 1):
                    pb = kt % 2
                    n_tr = g.add("tensor", lambda E, kt=kt: E.transpose(
                        ps_pt, p_sb[:, kt * 128:(kt + 1) * 128], ident_bf),
                        [n_exp, prev.get("pt_evac")])
                    n_pte = g.add("scalar", lambda E, pb=pb: E.copy(
                        pt_sb[pb], ps_pt), [n_tr, prev.get(("pt_rd", pb))])
                    prev["pt_evac"] = n_pte
                    for h in range(2):
                        pv_last = g.add("tensor", lambda E, kt=kt, h=h, pb=pb, i=i:
                                        E.matmul(
                            ps_o[:, h * 512:(h + 1) * 512], pt_sb[pb],
                            v_tm[:, kt * 1024 + h * 512: kt * 1024 + (h + 1) * 512],
                            start=(kt == 0), stop=(kt == i)),
                            [n_pte, prev.get("o_evac")] if kt == 0 and h == 0 else [n_pte])
                    prev[("pt_rd", pb)] = pv_last
                prev["pv_done"] = pv_last
                n_lx = g.dma("sync", xt[b], xin_t[:, t, :], [prev.get(("xt_rd", b))])
                n_o1 = g.add("vector", lambda E, b=b: E.tensor_scalar_mul(
                    o_sb[b], ps_o[:, 0:1024], lrow[:, 7:8]),
                    [pv_last, n_rl, prev.get(("o_rd", b))])
                prev["o_evac"] = n_o1
                n_o2 = g.add("vector", lambda E, b=b: E.tensor_add(
                    o_sb[b], o_sb[b], xt[b]), [n_o1, n_lx])
                prev[("xt_rd", b)] = n_o2
                n_st = g.dma("gpsimd", xout_t[:, t, :], o_sb[b], [n_o2])
                prev[("o_rd", b)] = n_st
            g.add("sync", lambda E: E.nop(),
                  [prev[("o_rd", 0)], prev[("o_rd", 1)]])
            g.lower()


def emit_l1ffn(nc, sems, counts, xin, xout, qw_ap, qs_ap, qb_ap, la_ap, lb_ap,
               lg_ap, lbeta_ap, scratch):
    xin_t = xin.rearrange("(t p) d -> p t d", p=128)
    xout_t = xout.rearrange("(t p) d -> p t d", p=128)
    hs = [scratch[k].rearrange("(t p) d -> p t d", p=128)
          for k in ("h0", "h1", "h2")]
    y1_t = scratch["y1"].rearrange("(t p) d -> p t d", p=128)
    with ExitStack() as ex:
        sb = lambda nm, sh, dt: ex.enter_context(nc.sbuf_tensor(nm, sh, dt))[:]
        psa = lambda nm, sh, dt=F32: ex.enter_context(nc.psum_tensor(nm, sh, dt))[:]

        ident_bf = sb("f1_id", [128, 128], BF16)
        ones_k1 = sb("f1_on1", [1, 128], BF16)
        cpool = sb("f1_cp", [128, 1], F32)
        wdqT = [sb(f"f1_wd{j}", [128, NC * 1024], BF16) for j in range(2)]
        laT = [sb(f"f1_la{j}", [128, 8 * 32], BF16) for j in range(6)]
        lbT = [sb(f"f1_lb{j}", [32, 1024], BF16) for j in range(6)]
        bias_b = [sb(f"f1_bb{j}", [1, 1024], BF16) for j in range(6)]
        gam_b = sb("f1_gb", [128, 1024], F32)
        bet_b = sb("f1_be", [128, 1024], F32)
        la_tmp = sb("f1_lat", [32, 1024], BF16)
        lb_tmp = sb("f1_lbt", [128, 8 * 32], BF16)
        q_i32 = [sb(f"f1_qi{j}", [128, 1024], I32) for j in range(2)]
        w_f32 = sb("f1_wf", [128, 1024], F32)
        w_bf = [sb(f"f1_wb{j}", [128, 1024], BF16) for j in range(2)]
        scl_sb = sb("f1_sc", [128, 8 * 64], F32)
        stats = sb("f1_st", [128, 64 * 2 + 32 * 6], F32)
        S1 = stats[:, 0:64]
        S2 = stats[:, 64:128]
        MUA = stats[:, 128:160]
        RSA = stats[:, 160:192]
        SCR0 = stats[:, 192:224]
        SCR1 = stats[:, 224:256]
        sqdump = [sb(f"f1_sqd{i}", [128, 1024], F32) for i in range(2)]
        ht = [sb(f"f1_ht{i}", [128, 1024], F32) for i in range(4)]
        hbf = [sb(f"f1_hb{i}", [128, 1024], BF16) for i in range(2)]
        hf = [sb(f"f1_hf{i}", [128, 1024], BF16) for i in range(4)]
        QT = 16
        y_big = sb("f1_yb", [128, QT * 1024], F32)
        g_sb = [sb(f"f1_g{i}", [128, 1024], F32) for i in range(2)]
        tT_bf = [sb(f"f1_tt{i}", [32, 128], BF16) for i in range(4)]
        out_sb = [sb(f"f1_o{i}", [128, 1024], F32) for i in range(2)]
        ps_tr = psa("f1_ptr", [128, 1024], BF16)
        ps_y = [psa(f"f1_py{i}", [128, 512]) for i in range(4)]
        ps_lt = [psa(f"f1_plt{i}", [128, 512]) for i in range(2)]
        ps_tr2 = psa("f1_ptr2", [128, 1024], BF16)

        n_bias = [None] * 6
        g = Graph(nc, sems, counts)
        emit_consts(g, dict(ident_bf=ident_bf))
        g.add("gpsimd", lambda E: E.memset(ones_k1, 1.0))
        g.add("gpsimd", lambda E: E.memset(cpool[:, 0:1], EPS_LN))
        prev_ev = None
        for j in range(6):
            n_la = g.dma("gpsimd", la_tmp[0:32, :], la_ap[j], [prev_ev])
            trs = []
            for c in range(NC):
                trs.append(g.add("tensor", lambda E, c=c: E.transpose(
                    ps_tr[:, c * 32:(c + 1) * 32], la_tmp[0:32, c * 128:(c + 1) * 128],
                    ident_bf[0:32, 0:32]),
                    [n_la, prev_ev] if c == 0 else []))
            n_ev1 = g.add("scalar", lambda E, j=j: E.copy(
                laT[j], ps_tr[:, 0:256]), [trs[-1]])
            n_lb = g.dma("gpsimd", lb_tmp.rearrange("p (c r) -> p c r", c=NC),
                         lb_ap[j].rearrange("(c p) r -> p c r", p=128), [n_ev1])
            trs = []
            for c in range(NC):
                trs.append(g.add("tensor", lambda E, c=c: E.transpose(
                    ps_tr[0:32, c * 128:(c + 1) * 128], lb_tmp[:, c * 32:(c + 1) * 32],
                    ident_bf), [n_lb, n_ev1] if c == 0 else []))
            n_ev2 = g.add("scalar", lambda E, j=j: E.copy(
                lbT[j], ps_tr[0:32, 0:1024]), [trs[-1]])
            prev_ev = n_ev2
            n_bias[j] = g.dma("gpsimd", bias_b[j], qb_ap[j].unsqueeze(0))
        g.lower(drain_gpsimd=False)

        def dequant_graph(blk):
            g = Graph(nc, sems, counts)
            qrd = {}
            prev_ev = None
            prev_scl = None
            for j in range(2):
                idx = 2 * blk + j
                n_sc = g.dma("sync", scl_sb.rearrange("p (r g2) -> p r g2", r=NC),
                             qs_ap[idx].rearrange("(r p) g2 -> p r g2", p=128),
                             [prev_scl])
                for r in range(NC):
                    qb_ = (j * NC + r) % 2
                    n_q = g.dma("gpsimd", q_i32[qb_],
                                qw_ap[idx, r * 128:(r + 1) * 128, :],
                                [qrd.get(qb_)])
                    n_cvt = g.add("vector", lambda E, qb_=qb_: E.tensor_scalar(
                        w_f32, q_i32[qb_], 1.0 / 7.5, -1.0, ALU.mult, ALU.add),
                        [n_q, prev_scl])
                    qrd[qb_] = n_cvt
                    n_scl = g.add("vector", lambda E, r=r, j=j: E.tensor_tensor(
                        w_bf[j].rearrange("p (g2 k) -> p g2 k", g2=64),
                        w_f32.rearrange("p (g2 k) -> p g2 k", g2=64),
                        scl_sb[:, r * 64:(r + 1) * 64].unsqueeze(2).broadcast_to(
                            [128, 64, 16]),
                        ALU.mult), [n_cvt, n_sc, prev_ev])
                    prev_scl = n_scl
                    n_ev = transpose_tile_nodes(
                        g, ps_tr, w_bf[j],
                        wdqT[j].rearrange("p (c o) -> p c o",
                                          c=NC)[:, :, r * 128:(r + 1) * 128],
                        ident_bf, [n_scl], evac_eng="scalar",
                        bank_free=prev_ev)
                    prev_ev = n_ev
            return g

        g = Graph(nc, sems, counts)
        n_prev_sq = None
        prev_ld = {}
        for t in range(NT):
            b = t % 2
            n_l = g.dma("sync", ht[b], xin_t[:, t, :], [prev_ld.get(b)])
            _, n_prev_sq = ln_stats_nodes(
                g, ht[b], S1[:, t:t + 1], S2[:, t:t + 1], sqdump[b], [n_l],
                sq_war=n_prev_sq)
            prev_ld[b] = n_prev_sq
        n_mu, n_rstd = ln_chain_nodes(
            g, S1[:, 0:32], S2[:, 0:32], MUA, RSA, SCR0, SCR1,
            [n_prev_sq, prev_ld[0], prev_ld[1]], cpool[:, 0:1])
        g.lower()
        g = Graph(nc, sems, counts)
        prev = {}
        for t in range(NT):
            b = t % 2
            n_l = g.dma("sync", ht[b], xin_t[:, t, :], [prev.get(("rd", b))])
            n_xn = g.add("vector", lambda E, b=b, t=t: E.tensor_scalar(
                out_sb[b], ht[b], MUA[:, t:t + 1], RSA[:, t:t + 1],
                ALU.subtract, ALU.mult), [n_l, prev.get(("st", b))])
            prev[("rd", b)] = n_xn
            n_st = g.dma("gpsimd", hs[0][:, t, :], out_sb[b], [n_xn])
            prev[("st", b)] = n_st
        g.add("sync", lambda E: E.nop(), [prev[("st", 0)], prev[("st", 1)]])
        g.lower()

        for blk in range(3):
            dequant_graph(blk).lower()
            h_in = hs[blk]
            i0, i1 = 2 * blk, 2 * blk + 1
            last = blk == 2
            h_out = xout_t if last else hs[blk + 1]

            g = Graph(nc, sems, counts)
            n_gam = g.dma("sync", gam_b,
                          lg_ap[blk].unsqueeze(0).partition_broadcast(128).squeeze(1))
            n_bet = g.dma("sync", bet_b,
                          lbeta_ap[blk].unsqueeze(0).partition_broadcast(128).squeeze(1))
            g.lower()

            for sh in range(NT // QT):
             t0, t1_ = sh * QT, (sh + 1) * QT
             g = Graph(nc, sems, counts)
             prev = {}

             def a_tail(t, i, b, b2, mm0, mm1):
                yb_i = y_big[:, i * 1024:(i + 1) * 1024]
                n_e = [None, None]
                for h, mmh in ((0, mm0), (1, mm1)):
                    n_e[h] = g.add("scalar", lambda E, h=h, i=i, b2=b2:
                                   E.activation(
                        y_big[:, i * 1024 + h * 512:i * 1024 + (h + 1) * 512],
                        ps_y[b2 * 2 + h], AF.Copy), [mmh])
                prev[("y_ev", b2)] = n_e[1]
                n_r1_ = g.add("vector", lambda E, yb_i=yb_i, t=t: E.reduce_sum(
                    S1[:, t:t + 1], yb_i, axis=AX.X), [n_e[0], n_e[1]])
                n_sqd = g.add("scalar", lambda E, yb_i=yb_i, b2=b2: E.activation(
                    sqdump[b2], yb_i, AF.Square),
                    [n_e[0], n_e[1], prev.get(("sqd_rd", b2))])
                n_r2_ = g.add("vector", lambda E, b2=b2, t=t: E.reduce_sum(
                    S2[:, t:t + 1], sqdump[b2], axis=AX.X), [n_sqd])
                prev[("sqd_rd", b2)] = n_r2_
                return n_r1_, n_r2_

             def a_mms(t, i, b, b2, n_ev):
                n_lt = None
                for c in range(NC):
                    n_lt = g.add("tensor", lambda E, c=c, b=b, b2=b2, i0=i0: E.matmul(
                        ps_lt[b2][0:32, 0:128], laT[i0][:, c * 32:(c + 1) * 32],
                        hf[b][:, c * 128:(c + 1) * 128],
                        start=(c == 0), stop=(c == NC - 1)),
                        [n_ev, prev.get(("lt_ev", b2))] if c == 0 else [])
                n_lte = g.add("scalar", lambda E, b=b, b2=b2: E.copy(
                    tT_bf[b], ps_lt[b2][0:32, 0:128]),
                    [n_lt, prev.get(("tT_rd", b))])
                prev[("lt_ev", b2)] = n_lte
                mm_last = [None, None]
                for c in range(NC):
                    for h in range(2):
                        dep0 = [n_ev, prev.get(("y_ev", b2))] if c == 0 else []
                        mm_last[h] = g.add("tensor", lambda E, c=c, h=h, b=b, b2=b2, j=0:
                                           E.matmul(
                            ps_y[b2 * 2 + h], hf[b][:, c * 128:(c + 1) * 128],
                            wdqT[j][:, c * 1024 + h * 512: c * 1024 + (h + 1) * 512],
                            start=(c == 0), stop=False), dep0)
                prev[("hf_rd", b)] = mm_last[1]
                for h in range(2):
                    mm_last[h] = g.add("tensor", lambda E, h=h, i0=i0, b=b, b2=b2: E.matmul(
                        ps_y[b2 * 2 + h], ones_k1, bias_b[i0][:, h * 512:(h + 1) * 512],
                        start=False, stop=False), [n_bias[i0]] if h == 0 else [])
                for h in range(2):
                    mm_last[h] = g.add("tensor", lambda E, h=h, i0=i0, b=b, b2=b2: E.matmul(
                        ps_y[b2 * 2 + h], tT_bf[b], lbT[i0][0:32, h * 512:(h + 1) * 512],
                        start=False, stop=True), [n_lte] if h == 0 else [])
                prev[("tT_rd", b)] = mm_last[1]
                return (t, i, b, b2, mm_last[0], mm_last[1])

             pend_mm = None
             pend_tail = None
             for t in range(t0, t1_):
                b = t % 4
                b2 = t % 2
                i = t - t0
                n_l = g.dma("sync", ht[b], h_in[:, t, :], [prev.get(("ht_rd", b))])
                n_cb = g.add("vector", lambda E, b=b, b2=b2: E.tensor_copy(
                    hbf[b2], ht[b]), [n_l, prev.get(("hbf_rd", b2))])
                prev[("ht_rd", b)] = n_cb
                n_ev = transpose_tile_nodes(
                    g, [ps_tr, ps_tr2][b2], hbf[b2], hf[b], ident_bf, [n_cb],
                    evac_eng="scalar", bank_free=prev.get(("tr_bank", b2)),
                    evac_deps=[prev.get(("hf_rd", b))])
                prev[("hbf_rd", b2)] = n_ev
                prev[("tr_bank", b2)] = n_ev
                if pend_mm is not None:
                    new_tail = a_mms(*pend_mm)
                    if pend_tail is not None:
                        r12 = a_tail(*pend_tail)
                    pend_tail = new_tail
                pend_mm = (t, i, b, b2, n_ev)
             new_tail = a_mms(*pend_mm)
             if pend_tail is not None:
                r12 = a_tail(*pend_tail)
             n_r1_, n_r2_ = a_tail(*new_tail)
             n_mu, n_rstd = ln_chain_nodes(
                g, S1[:, t0:t1_], S2[:, t0:t1_], MUA[:, t0:t1_], RSA[:, t0:t1_],
                stats[:, 256:256 + QT], stats[:, 288:288 + QT],
                [n_r1_, n_r2_], cpool[:, 0:1])
             g.lower()

             g = Graph(nc, sems, counts)
             prev = {}

             def b_tail(t, b, b2, mm0, mm1, n_lx):
                n_lh = g.dma("scalar", ht[b], h_in[:, t, :],
                             [prev.get(("ht_rd", b))])
                n_r = [None, None]
                for h, mmh in ((0, mm0), (1, mm1)):
                    n_r[h] = g.add("vector", lambda E, h=h, b=b, b2=b2:
                                   E.tensor_add(
                        out_sb[b2][:, h * 512:(h + 1) * 512], ps_y[b2 * 2 + h],
                        ht[b][:, h * 512:(h + 1) * 512]),
                        [mmh, n_lh, prev.get(("o_rd", b2))])
                prev[("y_ev", b2)] = n_r[1]
                prev[("ht_rd", b)] = n_r[1]
                st_deps = [n_r[0], n_r[1]]
                if last:
                    n_r2 = g.add("vector", lambda E, b2=b2: E.tensor_add(
                        out_sb[b2], out_sb[b2], sqdump[b2]),
                        [n_r[0], n_r[1], n_lx])
                    prev[("xf_rd", b2)] = n_r2
                    st_deps = [n_r2]
                prev[("o_rd", b2)] = g.dma(
                    "gpsimd", h_out[:, t, :], out_sb[b2], st_deps)

             def b_mms(t, b, b2, n_ev):
                n_lx = None
                if last:
                    n_lx = g.dma("scalar", sqdump[b2], xin_t[:, t, :],
                                 [prev.get(("xf_rd", b2))])
                n_lt = None
                for c in range(NC):
                    n_lt = g.add("tensor", lambda E, c=c, b=b, b2=b2, i1=i1: E.matmul(
                        ps_lt[b2][0:32, 0:128], laT[i1][:, c * 32:(c + 1) * 32],
                        hf[b][:, c * 128:(c + 1) * 128],
                        start=(c == 0), stop=(c == NC - 1)),
                        [n_ev, prev.get(("lt_ev", b2))] if c == 0 else [])
                n_lte = g.add("scalar", lambda E, b=b, b2=b2: E.copy(
                    tT_bf[b], ps_lt[b2][0:32, 0:128]),
                    [n_lt, prev.get(("tT_rd", b))])
                prev[("lt_ev", b2)] = n_lte
                mm_last = [None, None]
                for c in range(NC):
                    for h in range(2):
                        dep0 = [n_ev, prev.get(("y_ev", b2))] if c == 0 else []
                        mm_last[h] = g.add("tensor", lambda E, c=c, h=h, b=b, b2=b2:
                                           E.matmul(
                            ps_y[b2 * 2 + h], hf[b][:, c * 128:(c + 1) * 128],
                            wdqT[1][:, c * 1024 + h * 512: c * 1024 + (h + 1) * 512],
                            start=(c == 0), stop=False), dep0)
                prev[("hf_rd", b)] = mm_last[1]
                for h in range(2):
                    mm_last[h] = g.add("tensor", lambda E, h=h, i1=i1, b=b, b2=b2: E.matmul(
                        ps_y[b2 * 2 + h], ones_k1, bias_b[i1][:, h * 512:(h + 1) * 512],
                        start=False, stop=False), [n_bias[i1]] if h == 0 else [])
                for h in range(2):
                    mm_last[h] = g.add("tensor", lambda E, h=h, i1=i1, b=b, b2=b2: E.matmul(
                        ps_y[b2 * 2 + h], tT_bf[b], lbT[i1][0:32, h * 512:(h + 1) * 512],
                        start=False, stop=True), [n_lte] if h == 0 else [])
                prev[("tT_rd", b)] = mm_last[1]
                return (t, b, b2, mm_last[0], mm_last[1], n_lx)

             pend_mm = None
             pend_tail = None
             for t in range(t0, t1_):
                b = t % 4
                b2 = t % 2
                i = t - t0
                n_t1 = g.add("vector", lambda E, b2=b2, i=i, t=t: E.tensor_scalar(
                    g_sb[b2], y_big[:, i * 1024:(i + 1) * 1024],
                    MUA[:, t:t + 1], RSA[:, t:t + 1],
                    ALU.subtract, ALU.mult), [prev.get(("g_rd", b2))])
                n_t2 = g.add("vector", lambda E, b2=b2: E.tensor_mul(
                    g_sb[b2], g_sb[b2], gam_b), [n_t1, n_gam])
                n_t3 = g.add("vector", lambda E, b2=b2: E.tensor_add(
                    g_sb[b2], g_sb[b2], bet_b), [n_t2, n_bet])
                n_gl = g.add("scalar", lambda E, b2=b2: E.activation(
                    hbf[b2], g_sb[b2], AF.Gelu), [n_t3, prev.get(("hbf_rd", b2))])
                prev[("g_rd", b2)] = n_gl
                n_ev = transpose_tile_nodes(
                    g, [ps_tr, ps_tr2][b2], hbf[b2], hf[b], ident_bf, [n_gl],
                    evac_eng="scalar", bank_free=prev.get(("tr_bank", b2)),
                    evac_deps=[prev.get(("hf_rd", b))])
                prev[("hbf_rd", b2)] = n_ev
                prev[("tr_bank", b2)] = n_ev
                if pend_mm is not None:
                    new_tail = b_mms(*pend_mm)
                    if pend_tail is not None:
                        b_tail(*pend_tail)
                    pend_tail = new_tail
                pend_mm = (t, b, b2, n_ev)
             new_tail = b_mms(*pend_mm)
             if pend_tail is not None:
                b_tail(*pend_tail)
             b_tail(*new_tail)
             g.add("sync", lambda E: E.nop(),
                   [prev.get(("o_rd", bb)) for bb in range(2)])
             g.lower()


def build_l0ffn_program(debug=False):
    nc = bass.Bass("TRN2", target_bir_lowering=False)
    xin = nc.dram_tensor("x", [TOK, D], F32, kind="ExternalInput").ap()
    wr = nc.dram_tensor("wr", [D], F32, kind="ExternalInput").ap()
    wb = nc.dram_tensor("wb", [D, D], F32, kind="ExternalInput").ap()
    xout = nc.dram_tensor("xo", [TOK, D], F32, kind="ExternalOutput").ap()
    dbg = None
    if debug:
        dbg = {
            "scores": nc.dram_tensor("dbg_scores", [2, 128, 16], F32,
                                     kind="ExternalOutput").ap(),
            "tau": nc.dram_tensor("dbg_tau", [2, 128, 1], F32,
                                  kind="ExternalOutput").ap(),
            "minmax": nc.dram_tensor("dbg_minmax", [2, 1, 4], F32,
                                     kind="ExternalOutput").ap(),
            "xn": nc.dram_tensor("dbg_xn", [2, 128, 1024], F32,
                                 kind="ExternalOutput").ap(),
            "xf": nc.dram_tensor("dbg_xf", [2, 128, 1024], BF16,
                                 kind="ExternalOutput").ap(),
            "p": nc.dram_tensor("dbg_p", [2, 128, 1024], F32,
                                kind="ExternalOutput").ap(),
        }
    with ExitStack() as st:
        sems, counts = make_sems(nc, st)
        emit_l0ffn(nc, sems, counts, xin, xout, wr, wb, dbg=dbg)
    return nc


def build_l0attn_program():
    nc = bass.Bass("TRN2", target_bir_lowering=False)
    xin = nc.dram_tensor("x", [TOK, D], F32, kind="ExternalInput").ap()
    wq = nc.dram_tensor("wq", [D, D], F32, kind="ExternalInput").ap()
    wk = nc.dram_tensor("wk", [D, D], F32, kind="ExternalInput").ap()
    wv = nc.dram_tensor("wv", [D, D], F32, kind="ExternalInput").ap()
    wo = nc.dram_tensor("wo", [D, D], F32, kind="ExternalInput").ap()
    xout = nc.dram_tensor("xo", [TOK, D], F32, kind="ExternalOutput").ap()
    with ExitStack() as st:
        sems, counts = make_sems(nc, st)
        emit_l0attn(nc, sems, counts, xin, xout, wq, wk, wv, wo)
    return nc


def build_l1attn_program():
    nc = bass.Bass("TRN2", target_bir_lowering=False)
    xin = nc.dram_tensor("x", [TOK, D], F32, kind="ExternalInput").ap()
    wqkv = nc.dram_tensor("wqkv", [D, 3 * D], F32, kind="ExternalInput").ap()
    xout = nc.dram_tensor("xo", [TOK, D], F32, kind="ExternalOutput").ap()
    with ExitStack() as st:
        sems, counts = make_sems(nc, st)
        emit_l1attn(nc, sems, counts, xin, xout, wqkv)
    return nc


def build_l1ffn_program():
    nc = bass.Bass("TRN2", target_bir_lowering=False)
    xin = nc.dram_tensor("x", [TOK, D], F32, kind="ExternalInput").ap()
    qw = nc.dram_tensor("qw", [6, D, D], I32, kind="ExternalInput").ap()
    qs = nc.dram_tensor("qs", [6, D, 64], F32, kind="ExternalInput").ap()
    qb = nc.dram_tensor("qb", [6, D], F32, kind="ExternalInput").ap()
    la = nc.dram_tensor("la", [6, 32, D], F32, kind="ExternalInput").ap()
    lb = nc.dram_tensor("lb", [6, D, 32], F32, kind="ExternalInput").ap()
    lg = nc.dram_tensor("lg", [3, D], F32, kind="ExternalInput").ap()
    lbe = nc.dram_tensor("lbe", [3, D], F32, kind="ExternalInput").ap()
    xout = nc.dram_tensor("xo", [TOK, D], F32, kind="ExternalOutput").ap()
    scratch = {k: nc.dram_tensor("scr_" + k, [TOK, D], F32, kind="Internal").ap()
               for k in ("h0", "h1", "h2", "y1")}
    with ExitStack() as st:
        sems, counts = make_sems(nc, st)
        emit_l1ffn(nc, sems, counts, xin, xout, qw, qs, qb, la, lb, lg, lbe,
                   scratch)
    return nc


def build_fused_program():
    nc = bass.Bass("TRN2", target_bir_lowering=False)
    xin = nc.dram_tensor("x", [TOK, D], F32, kind="ExternalInput").ap()
    wq = nc.dram_tensor("wq", [D, D], F32, kind="ExternalInput").ap()
    wk = nc.dram_tensor("wk", [D, D], F32, kind="ExternalInput").ap()
    wv = nc.dram_tensor("wv", [D, D], F32, kind="ExternalInput").ap()
    wo = nc.dram_tensor("wo", [D, D], F32, kind="ExternalInput").ap()
    wr = nc.dram_tensor("wr", [D], F32, kind="ExternalInput").ap()
    wb = nc.dram_tensor("wb", [D, D], F32, kind="ExternalInput").ap()
    wqkv = nc.dram_tensor("wqkv", [D, 3 * D], F32, kind="ExternalInput").ap()
    qw = nc.dram_tensor("qw", [6, D, D], I32, kind="ExternalInput").ap()
    qs = nc.dram_tensor("qs", [6, D, 64], F32, kind="ExternalInput").ap()
    qb = nc.dram_tensor("qb", [6, D], F32, kind="ExternalInput").ap()
    la = nc.dram_tensor("la", [6, 32, D], F32, kind="ExternalInput").ap()
    lb = nc.dram_tensor("lb", [6, D, 32], F32, kind="ExternalInput").ap()
    lg = nc.dram_tensor("lg", [3, D], F32, kind="ExternalInput").ap()
    lbe = nc.dram_tensor("lbe", [3, D], F32, kind="ExternalInput").ap()
    xout = nc.dram_tensor("xo", [TOK, D], F32, kind="ExternalOutput").ap()
    xa = nc.dram_tensor("scr_xa", [TOK, D], F32, kind="Internal").ap()
    xb = nc.dram_tensor("scr_xb", [TOK, D], F32, kind="Internal").ap()
    scratch = {k: nc.dram_tensor("scr_" + k, [TOK, D], F32, kind="Internal").ap()
               for k in ("h0", "h1", "h2", "y1")}
    with ExitStack() as st:
        sems, counts = make_sems(nc, st)
        emit_l0attn(nc, sems, counts, xin, xa, wq, wk, wv, wo)
        emit_l0ffn(nc, sems, counts, xa, xb, wr, wb)
        emit_l1attn(nc, sems, counts, xb, xa, wqkv)
        emit_l1ffn(nc, sems, counts, xa, xout, qw, qs, qb, la, lb, lg, lbe,
                   scratch)
    return nc


_PROGRAMS = {}


def _get_program(name):
    if name not in _PROGRAMS:
        _PROGRAMS[name] = {
            "l0attn": build_l0attn_program,
            "l0ffn": build_l0ffn_program,
            "l1attn": build_l1attn_program,
            "l1ffn": build_l1ffn_program,
            "fused": build_fused_program,
        }[name]()
    return _PROGRAMS[name]


def kernel(**inputs):
    x = np.ascontiguousarray(np.asarray(inputs["L0_attn_x"], np.float32))
    w = {k: np.ascontiguousarray(np.asarray(v)) for k, v in inputs.items()
         if k != "L0_attn_x"}
    shards = [np.ascontiguousarray(x[2 * c:2 * c + 2].reshape(TOK, D))
              for c in range(8)]
    res = run_bass_kernel_spmd(
        _get_program("fused"),
        [{"x": s, "wq": w["L0_attn_Wq"], "wk": w["L0_attn_Wk"],
          "wv": w["L0_attn_Wv"], "wo": w["L0_attn_Wo"],
          "wr": w["L0_ffn_w_router"], "wb": w["L0_ffn_w_block"],
          "wqkv": w["L1_attn_W_qkv"], "qw": w["L1_ffn_qweights"],
          "qs": w["L1_ffn_scales"], "qb": w["L1_ffn_biases"],
          "la": w["L1_ffn_lora_a"], "lb": w["L1_ffn_lora_b"],
          "lg": w["L1_ffn_ln_gamma"], "lbe": w["L1_ffn_ln_beta"]}
         for s in shards],
        core_ids=list(range(8)))
    return np.concatenate([r["xo"].reshape(2, T, D) for r in res.results],
                          axis=0)


# version 14 (final)
# speedup vs baseline: 1.0753x; 1.0753x over previous
from contextlib import ExitStack

import numpy as np
import ml_dtypes
import concourse.bass as bass
import concourse.mybir as mybir
from concourse.bass_utils import run_bass_kernel_spmd

F32 = mybir.dt.float32
BF16 = mybir.dt.bfloat16
I32 = mybir.dt.int32
AF = mybir.ActivationFunctionType
ALU = mybir.AluOpType
AX = mybir.AxisListType

T = 2048
D = 1024
SEQ_PER_CORE = 2
TOK = SEQ_PER_CORE * T
NT = TOK // 128
NTS = T // 128
NC = D // 128
EPS_LN = 1e-5
EPS_HYP = 1e-6
K_CAP = 1536
NHEAD = 16
DH = 64
BISECT_ITERS = 36


NDMASTREAM = 4


class Node:
    __slots__ = ("eng", "emit", "deps", "is_dma", "needed", "val", "graph",
                 "stream")

    def __init__(self, eng, emit, deps, is_dma, graph):
        self.eng = eng
        self.emit = emit
        self.deps = [d for d in deps if d is not None]
        self.is_dma = is_dma
        self.needed = False
        self.val = None
        self.graph = graph
        self.stream = None


class Graph:
    CENG = ("tensor", "vector", "scalar", "gpsimd", "sync")

    def __init__(self, nc, sems, counts):
        self.nc = nc
        self.sems = sems
        self.counts = counts
        self.nodes = {e: [] for e in self.CENG}
        self.lowered = False
        self.counts0 = dict(counts)

    def add(self, eng, emit, deps=(), dma=False):
        n = Node(eng, emit, deps, dma, self)
        self.nodes[eng].append(n)
        return n

    def dma(self, eng, out, in_, deps=(), **kw):
        return self.add(eng, lambda E: E.dma_start(out, in_, **kw), deps, dma=True)

    def lower(self, drain_gpsimd=True):
        for e in self.CENG:
            for n in self.nodes[e]:
                for d in n.deps:
                    assert d.graph.lowered or d.graph is self, \
                        "dep on a LATER graph's node"
                    d.needed = True
        for e in self.CENG:
            always = e in ("vector", "scalar", "gpsimd")
            for n in self.nodes[e]:
                if n.is_dma:
                    s = self.counts["rr_" + e] % NDMASTREAM
                    self.counts["rr_" + e] += 1
                    key = f"d_{e}{s}"
                    n.stream = key
                    self.counts[key] += 16
                    n.val = self.counts[key]
                elif n.needed or always:
                    self.counts["c_" + e] += 1
                    n.val = self.counts["c_" + e]
        drains = []
        if drain_gpsimd:
            for s_ in range(NDMASTREAM):
                key = f"d_gpsimd{s_}"
                if self.counts[key] > self.counts0.get(key, 0):
                    drains.append((key, self.counts[key]))
        with self.nc.Block() as block:
            for e in self.CENG:
                if not self.nodes[e] and not (e == "sync" and drains):
                    continue

                def body(E, e=e):
                    hw = {}
                    for n in self.nodes[e]:
                        for d in n.deps:
                            key = d.stream if d.is_dma else ("c_" + d.eng)
                            assert d.val is not None, "dep on un-inc'd node"
                            if hw.get(key, 0) < d.val:
                                E.wait_ge(self.sems[key], d.val)
                                hw[key] = d.val
                        if n.is_dma and n.val > 16:
                            if hw.get(n.stream, 0) < n.val - 16:
                                E.wait_ge(self.sems[n.stream], n.val - 16)
                                hw[n.stream] = n.val - 16
                        inst = n.emit(E)
                        if n.is_dma:
                            inst.then_inc(self.sems[n.stream], 16)
                        elif n.val is not None:
                            inst.then_inc(self.sems["c_" + e], 1)
                    if e == "sync":
                        need_nop = not self.nodes[e]
                        for key, v in drains:
                            if hw.get(key, 0) < v:
                                E.wait_ge(self.sems[key], v)
                                hw[key] = v
                                need_nop = True
                        if need_nop:
                            E.nop()

                getattr(block, e)(body)
        self.lowered = True
        return self


def make_sems(nc, stack):
    sems = {}
    counts = {}
    for e in Graph.CENG:
        key = "c_" + e
        sems[key] = stack.enter_context(nc.semaphore(key))
        counts[key] = 0
        counts["rr_" + e] = 0
        for s in range(NDMASTREAM):
            key = f"d_{e}{s}"
            sems[key] = stack.enter_context(nc.semaphore(key))
            counts[key] = 0
    return sems, counts


def emit_consts(g, C):
    if "ident_bf" in C:
        n0 = g.add("gpsimd", lambda E: E.memset(C["ident_bf"], 0.0))
        g.add("gpsimd", lambda E: E.affine_select(
            out=C["ident_bf"], in_=C["ident_bf"], compare_op=ALU.not_equal,
            fill=1.0, base=0, pattern=[[-1, 128]], channel_multiplier=1), deps=[n0])
    if "ident_f" in C:
        n0 = g.add("gpsimd", lambda E: E.memset(C["ident_f"], 0.0))
        g.add("gpsimd", lambda E: E.affine_select(
            out=C["ident_f"], in_=C["ident_f"], compare_op=ALU.not_equal,
            fill=1.0, base=0, pattern=[[-1, 128]], channel_multiplier=1), deps=[n0])
    if "ones_col" in C:
        g.add("gpsimd", lambda E: E.memset(C["ones_col"], 1.0))
    if "ones_row" in C:
        g.add("gpsimd", lambda E: E.memset(C["ones_row"], 1.0))


def ln_stats_nodes(g, xt_ap, s1_ap, s2_ap, sqdump, deps, sq_war=None):
    n1 = g.add("vector", lambda E: E.reduce_sum(s1_ap, xt_ap, axis=AX.X), deps)
    nq = g.add("scalar", lambda E: E.activation(sqdump, xt_ap, AF.Square),
               list(deps) + ([sq_war] if sq_war else []))
    n2 = g.add("vector", lambda E: E.reduce_sum(s2_ap, sqdump, axis=AX.X), [nq])
    return n1, n2


def ln_chain_nodes(g, s1, s2, mu, rstd, scr0, scr1, deps, eps_ap, inv_n=1.0 / D):
    a = g.add("vector", lambda E: E.tensor_scalar_mul(mu, s1, inv_n), deps)
    b = g.add("vector", lambda E: E.tensor_scalar_mul(scr0, s2, inv_n), deps)
    c = g.add("vector", lambda E: E.tensor_mul(scr1, mu, mu), [a])
    d = g.add("vector", lambda E: E.tensor_sub(scr0, scr0, scr1), [b, c])
    e = g.add("scalar", lambda E: E.activation(scr1, scr0, AF.Ln, bias=eps_ap), [d])
    f = g.add("scalar", lambda E: E.activation(rstd, scr1, AF.Exp, scale=-0.5), [e])
    return a, f


def transpose_tile_nodes(g, ps_tr, src_bf, dst_ap, ident_bf, deps,
                         evac_eng="scalar", bank_free=None, n_chunks=8,
                         evac_deps=()):
    trs = []
    for c in range(n_chunks):
        dd = list(deps) + ([bank_free] if (bank_free is not None and c == 0) else [])
        trs.append(g.add("tensor", lambda E, c=c: E.transpose(
            ps_tr[:, c * 128:(c + 1) * 128], src_bf[:, c * 128:(c + 1) * 128],
            ident_bf), deps=dd if c == 0 else []))
    src = ps_tr[:, 0:n_chunks * 128]
    if len(dst_ap.shape) == 3:
        src = src.rearrange("p (c k) -> p c k", c=n_chunks)
    if evac_eng == "scalar":
        ev = g.add("scalar", lambda E: E.copy(dst_ap, src),
                   deps=[trs[-1]] + list(evac_deps))
    else:
        ev = g.add("vector", lambda E: E.tensor_copy(dst_ap, src),
                   deps=[trs[-1]] + list(evac_deps))
    return ev


def emit_l0ffn(nc, sems, counts, xin, xout, wr_ap, wb_ap, dbg=None):
    xin_t = xin.rearrange("(t p) d -> p t d", p=128)
    xout_t = xout.rearrange("(t p) d -> p t d", p=128)
    with ExitStack() as ex:
        sb = lambda nm, sh, dt: ex.enter_context(nc.sbuf_tensor(nm, sh, dt))[:]
        psa = lambda nm, sh, dt=F32: ex.enter_context(nc.psum_tensor(nm, sh, dt))[:]

        xt = sb("ff_xt", [128, NTS * 1024], F32)
        wb_sb = sb("ff_wb", [128, NC * 1024], BF16)
        wr_bc = sb("ff_wrb", [128, 1024], F32)
        ident_bf = sb("ff_id", [128, 128], BF16)
        ident_f = sb("ff_idf", [128, 128], F32)
        ones_row = sb("ff_onr", [1, 128], F32)
        ones_col = sb("ff_onc", [128, 1], F32)
        stats = sb("ff_st", [128, 16 * 8], F32)
        S = lambda k: stats[:, 16 * k:16 * (k + 1)]
        wrbar = sb("ff_wbar", [128, 1], F32)
        cpool = sb("ff_cp", [128, 2], F32)
        bis = sb("ff_bis", [1, 16], F32)
        BI = lambda k: bis[:, k:k + 1]
        tau_bc = sb("ff_tau", [128, 1], F32)
        mid_bc = sb("ff_mid", [128, 1], F32)
        sqdump = sb("ff_sqd", [128, 1024], F32)
        xw_tmp = sb("ff_xwt", [128, 1024], F32)
        gt = sb("ff_gt", [128, 16], F32)
        xn = [sb(f"ff_xn{i}", [128, 1024], F32) for i in range(2)]
        xnbf = [sb(f"ff_xnb{i}", [128, 1024], BF16) for i in range(2)]
        xf = [sb(f"ff_xf{i}", [128, 1024], BF16) for i in range(2)]
        p_sb = [sb(f"ff_p{i}", [128, 1024], F32) for i in range(2)]
        dd_sb = [sb(f"ff_dd{i}", [128, 1024], F32) for i in range(2)]
        m_col = sb("ff_mc", [128, 2], F32)
        out_sb = [sb(f"ff_o{i}", [128, 1024], F32) for i in range(2)]
        ps_tr = psa("ff_ptr", [128, 1024], BF16)
        ps_p = [psa("ff_pp0", [128, 512]), psa("ff_pp1", [128, 512])]
        ps_cnt = psa("ff_pcnt", [128, 512])
        C = dict(ident_bf=ident_bf, ident_f=ident_f, ones_row=ones_row,
                 ones_col=ones_col)

        g = Graph(nc, sems, counts)
        emit_consts(g, C)
        g.add("gpsimd", lambda E: E.memset(cpool[:, 0:1], EPS_LN))
        n_wb = g.dma("gpsimd", wb_sb.rearrange("p (c e) -> p c e", c=NC),
                     wb_ap.rearrange("(c p) e -> p c e", p=128))
        n_wr = g.dma("sync", wr_bc,
                     wr_ap.unsqueeze(0).partition_broadcast(128).squeeze(1))
        g.add("vector", lambda E: E.reduce_sum(wrbar, wr_bc, axis=AX.X), [n_wr])
        g.lower(drain_gpsimd=False)

        for s in range(SEQ_PER_CORE):
            g = Graph(nc, sems, counts)
            n_prev_xw = None
            n_prev_sq = None
            loads = []
            for i in range(NTS):
                t = s * NTS + i
                xt_i = xt[:, i * 1024:(i + 1) * 1024]
                n_l = g.dma("sync", xt_i, xin_t[:, t, :])
                loads.append(n_l)
                _, n_prev_sq = ln_stats_nodes(
                    g, xt_i, S(0)[:, i:i + 1], S(1)[:, i:i + 1], sqdump,
                    [n_l], sq_war=n_prev_sq)
                n_x1 = g.add("vector", lambda E, xt_i=xt_i: E.tensor_mul(
                    xw_tmp, xt_i, wr_bc),
                    [n_l, n_wr] + ([n_prev_xw] if n_prev_xw else []))
                n_prev_xw = g.add("vector", lambda E, i=i: E.reduce_sum(
                    S(4)[:, i:i + 1], xw_tmp, axis=AX.X), [n_x1])
            n_last_load = loads[-1]
            dep_all = [n_prev_xw, n_prev_sq] + loads
            n_mu, n_rstd = ln_chain_nodes(g, S(0), S(1), S(2), S(3), S(6), S(7),
                                          dep_all, cpool[:, 0:1])
            n_a = g.add("vector", lambda E: E.tensor_scalar_mul(S(6), S(2), wrbar),
                        [n_mu, n_rstd])
            n_b = g.add("vector", lambda E: E.tensor_sub(S(7), S(4), S(6)), [n_a])
            n_s = g.add("vector", lambda E: E.tensor_mul(S(5), S(7), S(3)),
                        [n_b, n_rstd])
            n_p1 = g.add("vector", lambda E: E.reduce_max(S(6)[:, 0:1], S(5),
                                                          axis=AX.X), [n_s])
            n_p2 = g.add("vector", lambda E: E.tensor_reduce(S(6)[:, 1:2], S(5),
                                                             axis=AX.X, op=ALU.min),
                        [n_s])
            n_t1 = g.add("tensor", lambda E: E.transpose(
                ps_cnt[0:1, 0:128], S(6)[:, 0:1], ident_f), [n_p1, n_p2])
            n_t2 = g.add("tensor", lambda E: E.transpose(
                ps_cnt[0:1, 128:256], S(6)[:, 1:2], ident_f), [n_p1, n_p2])
            n_mx = g.add("vector", lambda E: E.reduce_max(BI(0), ps_cnt[0:1, 0:128],
                                                          axis=AX.X), [n_t1, n_t2])
            n_mn = g.add("vector", lambda E: E.tensor_reduce(
                BI(1), ps_cnt[0:1, 128:256], axis=AX.X, op=ALU.min), [n_t1, n_t2])
            n_lo = g.add("vector", lambda E: E.tensor_scalar_add(BI(2), BI(1), -1.0),
                         [n_mn])
            n_hi = g.add("vector", lambda E: E.tensor_scalar_add(BI(3), BI(0), 1.0),
                         [n_mx])
            n_md = g.add("vector", lambda E: E.tensor_add(BI(4), BI(2), BI(3)),
                         [n_lo, n_hi])
            n_md = g.add("vector", lambda E: E.tensor_scalar_mul(BI(4), BI(4), 0.5),
                         [n_md])
            last_cnt_reader = None
            last_mid_reader = None
            for it in range(BISECT_ITERS):
                n_bc = g.add("tensor", lambda E: E.matmul(
                    ps_cnt[:, 256:257], ones_row, BI(4), start=True, stop=True),
                    [n_md] + ([last_mid_reader] if last_mid_reader else []))
                n_mb = g.add("vector", lambda E: E.tensor_copy(mid_bc, ps_cnt[:, 256:257]),
                             [n_bc])
                last_mid_reader = n_mb
                n_gt = g.add("vector", lambda E: E.tensor_single_scalar(
                    gt, S(5), mid_bc, ALU.is_gt), [n_mb])
                n_ct = g.add("tensor", lambda E: E.matmul(
                    ps_cnt[0:1, 384:400], ones_col, gt, start=True, stop=True),
                    [n_gt] + ([last_cnt_reader] if last_cnt_reader else []))
                n_rd = g.add("vector", lambda E: E.reduce_sum(
                    BI(5), ps_cnt[0:1, 384:400], axis=AX.X), [n_ct])
                last_cnt_reader = n_rd
                n_pr = g.add("vector", lambda E: E.tensor_single_scalar(
                    BI(6), BI(5), float(K_CAP), ALU.is_ge), [n_rd])
                n_d1 = g.add("vector", lambda E: E.tensor_sub(BI(7), BI(4), BI(2)),
                             [n_pr])
                n_d2 = g.add("vector", lambda E: E.tensor_mul(BI(7), BI(7), BI(6)),
                             [n_d1])
                n_lo2 = g.add("vector", lambda E: E.tensor_add(BI(2), BI(2), BI(7)),
                              [n_d2])
                n_e1 = g.add("vector", lambda E: E.tensor_sub(BI(8), BI(3), BI(4)),
                             [n_pr])
                n_e2 = g.add("vector", lambda E: E.tensor_mul(BI(8), BI(8), BI(6)),
                             [n_e1])
                n_hi2 = g.add("vector", lambda E: E.tensor_add(BI(3), BI(4), BI(8)),
                              [n_e2, n_lo2])
                n_md = g.add("vector", lambda E: E.tensor_add(BI(4), BI(2), BI(3)),
                             [n_lo2, n_hi2])
                n_md = g.add("vector", lambda E: E.tensor_scalar_mul(
                    BI(4), BI(4), 0.5), [n_md])
            n_tb = g.add("tensor", lambda E: E.matmul(
                ps_cnt[:, 256:257], ones_row, BI(2), start=True, stop=True),
                [n_md, last_mid_reader])
            n_tau = g.add("vector", lambda E: E.tensor_copy(tau_bc, ps_cnt[:, 256:257]),
                          [n_tb])
            if dbg is not None:
                d1 = g.dma("sync", dbg["scores"][s, :, :], S(5), [n_s])
                d2 = g.dma("sync", dbg["tau"][s, :, :], tau_bc, [n_tau])
                d3 = g.dma("sync", dbg["minmax"][s, :, :], bis[0:1, 0:4], [n_tau])
                g.add("sync", lambda E: E.nop(), [d1, d2, d3])
            g.lower()

            g = Graph(nc, sems, counts)
            prev = {}
            for i in range(NTS):
                t = s * NTS + i
                b = i % 2
                xt_i = xt[:, i * 1024:(i + 1) * 1024]
                n_xn = g.add("vector", lambda E, xt_i=xt_i, b=b, i=i: E.tensor_scalar(
                    xn[b], xt_i, S(2)[:, i:i + 1], S(3)[:, i:i + 1],
                    ALU.subtract, ALU.mult),
                    [prev.get(("xn_rd", b)), n_last_load])
                n_cb = g.add("vector", lambda E, b=b: E.tensor_copy(xnbf[b], xn[b]),
                             [n_xn, prev.get(("xnbf_rd", b))])
                n_ev = transpose_tile_nodes(
                    g, ps_tr, xnbf[b], xf[b], ident_bf, [n_cb],
                    evac_eng="scalar", bank_free=prev.get("tr_bank"),
                    evac_deps=[prev.get(("xf_rd", b))])
                prev[("xnbf_rd", b)] = n_ev
                prev["tr_bank"] = n_ev
                mm_last = [None, None]
                for c in range(NC):
                    for h in range(2):
                        dep0 = [n_ev, prev.get("gelu_any"), n_wb]
                        mm_last[h] = g.add("tensor", lambda E, c=c, h=h, b=b: E.matmul(
                            ps_p[h], xf[b][:, c * 128:(c + 1) * 128],
                            wb_sb[:, c * 1024 + h * 512: c * 1024 + (h + 1) * 512],
                            start=(c == 0), stop=(c == NC - 1)),
                            dep0 if c == 0 else [])
                prev[("xf_rd", b)] = mm_last[1]
                n_g0 = g.add("scalar", lambda E, b=b: E.activation(
                    p_sb[b][:, 0:512], ps_p[0], AF.Gelu_apprx_tanh),
                    [mm_last[0], prev.get(("p_rd", b))])
                n_g1 = g.add("scalar", lambda E, b=b: E.activation(
                    p_sb[b][:, 512:1024], ps_p[1], AF.Gelu_apprx_tanh),
                    [mm_last[1]])
                prev[("gelu", b)] = n_g1
                prev["gelu_any"] = n_g1
                n_mc = g.add("vector", lambda E, i=i, b=b: E.tensor_single_scalar(
                    m_col[:, b:b + 1], S(5)[:, i:i + 1], tau_bc, ALU.is_gt),
                    [n_tau, prev.get(("mc_rd", b))])
                n_d = g.add("vector", lambda E, b=b: E.tensor_sub(
                    dd_sb[b], p_sb[b], xn[b]), [n_g0, n_g1, n_xn])
                prev[("p_rd", b)] = n_d
                n_dm = g.add("vector", lambda E, b=b: E.tensor_scalar_mul(
                    dd_sb[b], dd_sb[b], m_col[:, b:b + 1]), [n_d, n_mc])
                prev[("mc_rd", b)] = n_dm
                n_r1 = g.add("vector", lambda E, xt_i=xt_i, b=b: E.tensor_add(
                    out_sb[b], xt_i, xn[b]), [n_xn, prev.get(("out_st", b))])
                prev[("xn_rd", b)] = n_r1
                n_r2 = g.add("vector", lambda E, b=b: E.tensor_add(
                    out_sb[b], out_sb[b], dd_sb[b]), [n_r1, n_dm])
                n_st = g.dma("gpsimd", xout_t[:, t, :], out_sb[b], [n_r2])
                prev[("out_st", b)] = n_st
                if dbg is not None and s == 0 and i < 2:
                    dx = g.dma("sync", dbg["xn"][i, :, :], xn[b], [n_xn, n_d])
                    df = g.dma("sync", dbg["xf"][i, :, :], xf[b], [n_ev, mm_last[1]])
                    dp = g.dma("sync", dbg["p"][i, :, :], p_sb[b], [n_g1, n_d])
                    prev[("xn_rd", b)] = dx
                    prev[("xf_rd", b)] = df
                    prev[("p_rd", b)] = dp
            g.add("sync", lambda E: E.nop(), [prev[("out_st", 0)], prev[("out_st", 1)]])
            g.lower()


def emit_l0attn(nc, sems, counts, xin, xout, wq_ap, wk_ap, wv_ap, wo_ap):
    xin_t = xin.rearrange("(t p) d -> p t d", p=128)
    xout_t = xout.rearrange("(t p) d -> p t d", p=128)
    HW = 65
    with ExitStack() as ex:
        sb = lambda nm, sh, dt: ex.enter_context(nc.sbuf_tensor(nm, sh, dt))[:]
        psa = lambda nm, sh, dt=F32: ex.enter_context(nc.psum_tensor(nm, sh, dt))[:]

        ident_bf = sb("a0_id", [128, 128], BF16)
        ones_row = sb("a0_onr", [1, 128], F32)
        cpool = sb("a0_cp", [128, 3], F32)
        q_fm = sb("a0_q", [128, NC * T], BF16)
        k_fm = sb("a0_k", [128, NC * T], BF16)
        v_aug = sb("a0_v", [128, NTS * NHEAD * HW], BF16)
        att_qc = sb("a0_att", [128, NC * 512], BF16)
        buf_a = [sb(f"a0_ba{i}", [128, 1024], F32) for i in range(2)]
        xn_scr = sb("a0_xn", [128, 1024], F32)
        xn_scr2 = sb("a0_xn2", [128, 1024], F32)
        spt = sb("a0_sp", [128, 1024], F32)
        sqdump = sb("a0_sqd", [128, 1024], F32)
        bf_scr = [sb(f"a0_bf{i}", [128, 1024], BF16) for i in range(3)]
        bf_scr2 = [sb(f"a0_bg{i}", [128, 1024], BF16) for i in range(2)]
        vfm = [sb(f"a0_vf{i}", [128, 1024], BF16) for i in range(2)]
        pT = [sb(f"a0_pt{i}", [128, 512], BF16) for i in range(2)]
        rb_sb = sb("a0_rb", [128, 512], F32)
        rl_row = sb("a0_rl", [1, 1024], F32)
        sm = [sb(f"a0_sm{i}", [128, 24], F32) for i in range(2)]
        gst = sb("a0_gst", [128, 16 * 12], F32)
        hx = [sb(f"a0_hx{i}", [128, 32], F32) for i in range(2)]

        g = Graph(nc, sems, counts)
        emit_consts(g, dict(ident_bf=ident_bf, ones_row=ones_row))
        g.add("gpsimd", lambda E: E.memset(cpool[:, 0:1], EPS_LN))
        g.add("gpsimd", lambda E: E.memset(cpool[:, 1:2], 1.0 + EPS_HYP))
        g.add("gpsimd", lambda E: E.memset(cpool[:, 2:3], EPS_HYP))
        g.add("gpsimd", lambda E: E.memset(
            v_aug.rearrange("p (t h m) -> p t h m", h=NHEAD, m=HW)[:, :, :, 64:65],
            1.0))
        g.lower()

        def weight_graph(dst_list, ap_list, stage, ps_tr):
            g = Graph(nc, sems, counts)
            prev_ev = None
            prev_ld = {}
            for dst, ap in zip(dst_list, ap_list):
                dst_v = dst.rearrange("p (c o) -> p c o", c=NC)
                for rr in range(NC):
                    b = rr % 2
                    n_l = g.dma("gpsimd", stage[b], ap[rr * 128:(rr + 1) * 128, :],
                                [prev_ld.get(b)])
                    n_ev = transpose_tile_nodes(
                        g, ps_tr, stage[b], dst_v[:, :, rr * 128:(rr + 1) * 128],
                        ident_bf, [n_l], evac_eng="scalar", bank_free=prev_ev)
                    prev_ld[b] = n_ev
                    prev_ev = n_ev
            g.last_ev = prev_ev
            g.lower()
            return g

        CH1 = cpool[:, 1:2]
        CH2 = cpool[:, 2:3]

        for s in range(SEQ_PER_CORE):
            with ExitStack() as exw:
                sbw = lambda nm, sh, dt: exw.enter_context(
                    nc.sbuf_tensor(nm, sh, dt, side="right"))[:]
                stage = [sbw(f"a0_st{s}{i}", [128, 1024], BF16) for i in range(2)]
                wqT = sbw(f"a0_wq{s}", [128, NC * 1024], BF16)
                wkT = sbw(f"a0_wk{s}", [128, NC * 1024], BF16)
                wvT = sbw(f"a0_wv{s}", [128, NC * 1024], BF16)
                ps_tr = exw.enter_context(
                    nc.psum_tensor(f"a0_ptr{s}", [128, 1024], BF16))[:]
                ps_trB = exw.enter_context(
                    nc.psum_tensor(f"a0_ptrB{s}", [128, 1024], BF16))[:]
                psb = [exw.enter_context(
                    nc.psum_tensor(f"a0_p{s}{i}", [128, 512], F32))[:]
                    for i in range(6)]
                n_wld = weight_graph([wqT, wkT, wvT], [wq_ap, wk_ap, wv_ap],
                                     stage, ps_tr).last_ev

                GS = lambda k: gst[:, 16 * k:16 * (k + 1)]
                g = Graph(nc, sems, counts)
                prev = {}
                for i in range(NTS):
                    t = s * NTS + i
                    b = i % 2
                    n_l = g.dma("sync", buf_a[b], xin_t[:, t, :],
                                [prev.get(("xt_rd", b)), prev.get(("xt_rd2", b))])
                    n_s1 = g.add("vector", lambda E, b=b, i=i: E.reduce_sum(
                        GS(0)[:, i:i + 1], buf_a[b], axis=AX.X), [n_l])
                    n_sq_ = g.add("scalar", lambda E, b=b: E.activation(
                        sqdump, buf_a[b], AF.Square),
                        [n_l, prev.get("sq_war")])
                    n_s2 = g.add("vector", lambda E, i=i: E.reduce_sum(
                        GS(1)[:, i:i + 1], sqdump, axis=AX.X), [n_sq_])
                    prev["sq_war"] = n_s2
                    prev[("xt_rd", b)] = n_s1
                    prev[("xt_rd2", b)] = n_sq_
                n_mu = g.add("vector", lambda E: E.tensor_scalar_mul(
                    GS(2), GS(0), 1.0 / D), [n_s2])
                n_e2_ = g.add("vector", lambda E: E.tensor_scalar_mul(
                    GS(7), GS(1), 1.0 / D), [n_s2])
                n_m2 = g.add("vector", lambda E: E.tensor_mul(GS(8), GS(2), GS(2)),
                             [n_mu])
                n_vr = g.add("vector", lambda E: E.tensor_sub(GS(7), GS(7), GS(8)),
                             [n_e2_, n_m2])
                n_ln = g.add("scalar", lambda E: E.activation(
                    GS(8), GS(7), AF.Ln, bias=cpool[:, 0:1]), [n_vr, n_sq_])
                n_rs = g.add("scalar", lambda E: E.activation(
                    GS(3), GS(8), AF.Exp, scale=-0.5), [n_ln])
                for i in range(NTS):
                    t = s * NTS + i
                    b = i % 2
                    n_l2 = g.dma("sync", buf_a[b], xin_t[:, t, :],
                                 [prev.get(("xt_rd", b)), prev.get(("xt_rd2", b))])
                    n_xn = g.add("vector", lambda E, b=b, i=i: E.tensor_scalar(
                        xn_scr, buf_a[b], GS(2)[:, i:i + 1], GS(3)[:, i:i + 1],
                        ALU.subtract, ALU.mult),
                        [n_l2, n_rs, prev.get("xn_rd")])
                    prev[("xt_rd", b)] = n_xn
                    prev[("xt_rd2", b)] = n_xn
                    n_sp = g.add("vector", lambda E: E.tensor_scalar(
                        spt[:, 1:1024], xn_scr[:, 1:1024], 8.0, -8.0,
                        ALU.min, ALU.max), [n_xn, prev.get("sp_rd")])
                    prev["xn_rd"] = n_sp
                    n_sq2_ = g.add("scalar", lambda E: E.activation(
                        sqdump[:, 1:1024], spt[:, 1:1024], AF.Square),
                        [n_sp, prev.get("sq_war")])
                    prev["sp_rd"] = n_sq2_
                    n_ssp = g.add("vector", lambda E, i=i: E.reduce_sum(
                        GS(4)[:, i:i + 1], sqdump[:, 1:1024], axis=AX.X), [n_sq2_])
                    prev["sq_war"] = n_ssp
                n = g.add("scalar", lambda E: E.activation(
                    GS(7), GS(4), AF.Ln, bias=CH1), [n_ssp, n_sq2_])
                n = g.add("scalar", lambda E: E.activation(
                    GS(8), GS(7), AF.Exp, scale=0.5), [n])
                n_z = g.add("vector", lambda E: E.tensor_single_scalar(
                    GS(9), GS(8), CH1, ALU.max), [n])
                n = g.add("scalar", lambda E: E.activation(
                    GS(7), GS(9), AF.Square), [n_z])
                n = g.add("vector", lambda E: E.tensor_scalar_add(
                    GS(7), GS(7), -1.0), [n])
                n = g.add("scalar", lambda E: E.activation(
                    GS(10), GS(7), AF.Ln), [n])
                n = g.add("scalar", lambda E: E.activation(
                    GS(10), GS(10), AF.Exp, scale=0.5), [n])
                n = g.add("vector", lambda E: E.tensor_add(
                    GS(10), GS(9), GS(10)), [n])
                n = g.add("scalar", lambda E: E.activation(
                    GS(10), GS(10), AF.Ln), [n])
                n_d = g.add("vector", lambda E: E.tensor_single_scalar(
                    GS(10), GS(10), 0.001, ALU.max), [n])
                n_d0 = g.add("vector", lambda E: E.tensor_sub(
                    GS(11), GS(8), GS(9)), [n_z, n_d])
                n = g.add("scalar", lambda E: E.activation(
                    GS(7), GS(11), AF.Square), [n_d0])
                n = g.add("vector", lambda E: E.tensor_sub(GS(7), GS(4), GS(7)),
                          [n])
                n = g.add("vector", lambda E: E.tensor_single_scalar(
                    GS(7), GS(7), CH2, ALU.max), [n])
                n = g.add("scalar", lambda E: E.activation(
                    GS(7), GS(7), AF.Ln), [n])
                n = g.add("scalar", lambda E: E.activation(
                    GS(7), GS(7), AF.Exp, scale=0.5), [n])
                n = g.add("vector", lambda E: E.tensor_single_scalar(
                    GS(7), GS(7), CH2, ALU.add), [n])
                n = g.add("vector", lambda E: E.reciprocal(GS(7), GS(7)), [n])
                n_scl = g.add("vector", lambda E: E.tensor_mul(
                    GS(5), GS(10), GS(7)), [n, n_d])
                n_v0 = g.add("vector", lambda E: E.tensor_mul(
                    GS(6), GS(11), GS(5)), [n_scl])
                def emit_qk_tr(pi, pb3, pb2, pcq, pck):
                    n_evq = transpose_tile_nodes(
                        g, ps_trB, bf_scr[pb3],
                        q_fm.rearrange("p (c t2) -> p c t2",
                                       c=NC)[:, :, pi * 128:(pi + 1) * 128],
                        ident_bf, [pcq], evac_eng="scalar",
                        bank_free=prev.get("tr_bankB"))
                    prev[("bf_rd", pb3)] = n_evq
                    n_evk = transpose_tile_nodes(
                        g, ps_trB, bf_scr2[pb2],
                        k_fm.rearrange("p (c t2) -> p c t2",
                                       c=NC)[:, :, pi * 128:(pi + 1) * 128],
                        ident_bf, [pck], evac_eng="scalar",
                        bank_free=n_evq)
                    prev[("bf2_rd", pb2)] = n_evk
                    prev["tr_bankB"] = n_evk

                def emit_projh(pi, pb, pb3, pvev, clq0, clq1, clk0, clk1):
                    casts = [None, None]
                    for which in range(2):
                        buf = xn_scr if which == 0 else xn_scr2
                        cl0, cl1 = (clq0, clq1) if which == 0 else (clk0, clk1)
                        n_qs = g.add("scalar", lambda E, buf=buf: E.activation(
                            sqdump, buf, AF.Square), [cl1, cl0, prev["sq_war"]])
                        prev["sq_war"] = n_qs
                        n_hs = g.add("vector", lambda E, pb=pb: E.reduce_sum(
                            hx[pb][:, 0:16],
                            sqdump.rearrange("p (h k) -> p h k", h=16),
                            axis=AX.X), [n_qs, prev.get(("hx_rd", pb))])
                        n_cq = g.add("vector", lambda E, pb=pb, buf=buf:
                                     E.tensor_tensor(
                            sqdump[:, 0:16], buf[:, 0:1024:64],
                            buf[:, 0:1024:64], ALU.mult), [n_hs])
                        n_sub = g.add("vector", lambda E, pb=pb: E.tensor_sub(
                            hx[pb][:, 16:32], hx[pb][:, 0:16], sqdump[:, 0:16]),
                            [n_hs, n_cq])
                        n_x1 = g.add("scalar", lambda E, pb=pb: E.activation(
                            hx[pb][:, 0:16], hx[pb][:, 16:32], AF.Ln, bias=CH1),
                            [n_sub])
                        n_x2 = g.add("scalar", lambda E, pb=pb: E.activation(
                            hx[pb][:, 16:32], hx[pb][:, 0:16], AF.Exp, scale=0.5),
                            [n_x1])
                        sgn = -1.0 if which == 0 else 1.0
                        n_wr = g.add("vector", lambda E, pb=pb, sgn=sgn, buf=buf:
                                     E.tensor_scalar_mul(
                            buf[:, 0:1024:64], hx[pb][:, 16:32], sgn), [n_x2])
                        prev[("hx_rd", pb)] = n_wr
                        if which == 0:
                            n_cast = g.add("vector", lambda E, pb3=pb3:
                                           E.tensor_copy(bf_scr[pb3], xn_scr),
                                           [n_wr, pvev])
                            prev["xn_rd"] = n_cast
                        else:
                            n_cast = g.add("vector", lambda E, pb=pb:
                                           E.tensor_copy(bf_scr2[pb], xn_scr2),
                                           [n_wr, prev.get(("bf2_rd", pb))])
                            prev["xn2_rd"] = n_cast
                        casts[which] = n_cast
                    return casts

                chain_out = {}

                def emit_chain(i):
                    t = s * NTS + i
                    b = i % 2
                    b3 = i % 3
                    n_l3 = g.dma("sync", buf_a[b], xin_t[:, t, :],
                                 [prev.get(("xt_rd", b)),
                                  prev.get(("xt_rd2", b))])
                    n_xn3 = g.add("vector", lambda E, b=b, i=i: E.tensor_scalar(
                        spt, buf_a[b], GS(2)[:, i:i + 1], GS(3)[:, i:i + 1],
                        ALU.subtract, ALU.mult),
                        [n_l3, n_v0, prev.get("sp_rd")])
                    prev[("xt_rd", b)] = n_xn3
                    prev[("xt_rd2", b)] = n_xn3
                    n_sp3 = g.add("vector", lambda E: E.tensor_scalar(
                        spt[:, 1:1024], spt[:, 1:1024], 8.0, -8.0,
                        ALU.min, ALU.max), [n_xn3])
                    n_vb = g.add("vector", lambda E, b3=b3, i=i:
                                 E.tensor_scalar_mul(
                        bf_scr[b3][:, 1:1024], spt[:, 1:1024],
                        GS(5)[:, i:i + 1]),
                        [n_sp3, prev.get(("bf_rd", b3))])
                    prev["sp_rd"] = n_vb
                    n_vc = g.add("vector", lambda E, b3=b3, i=i: E.tensor_copy(
                        bf_scr[b3][:, 0:1], GS(6)[:, i:i + 1]), [n_vb])
                    chain_out[i] = (b, b3, n_vb, n_vc)

                emit_chain(0)
                pend_tail = None
                for i in range(NTS):
                    b, b3, n_vb, n_vc = chain_out.pop(i)
                    if i + 1 < NTS:
                        emit_chain(i + 1)
                    n_ev = transpose_tile_nodes(
                        g, ps_tr, bf_scr[b3], vfm[b], ident_bf, [n_vb, n_vc],
                        evac_eng="scalar", bank_free=prev.get("tr_bankA"),
                        evac_deps=[prev.get(("vfm_rd", b))])
                    prev[("bf_rd", b3)] = n_ev
                    prev["tr_bankA"] = n_ev
                    mm = [None] * 6
                    for c in range(NC):
                        for wi, wT in enumerate((wqT, wkT, wvT)):
                            for h in range(2):
                                k6 = wi * 2 + h
                                dep0 = []
                                if c == 0:
                                    dep0 = [n_ev, prev.get(("bank_rd", k6)), n_wld]
                                mm[k6] = g.add("tensor",
                                               lambda E, c=c, wT=wT, h=h, b=b, k6=k6:
                                               E.matmul(
                                    psb[k6], vfm[b][:, c * 128:(c + 1) * 128],
                                    wT[:, c * 1024 + h * 512: c * 1024 + (h + 1) * 512],
                                    start=(c == 0), stop=(c == NC - 1)), dep0)
                    prev[("vfm_rd", b)] = mm[5]
                    if pend_tail is not None:
                        pcq, pck = emit_projh(*pend_tail)
                        emit_qk_tr(pend_tail[0], pend_tail[2],
                                   pend_tail[1], pcq, pck)
                    n_q0q = g.add("vector", lambda E: E.tensor_scalar(
                        xn_scr[:, 0:512], psb[0], 5.0, -5.0, ALU.min, ALU.max),
                        [mm[0], prev.get("xn_rd")])
                    n_q1q = g.add("vector", lambda E: E.tensor_scalar(
                        xn_scr[:, 512:1024], psb[1], 5.0, -5.0, ALU.min, ALU.max),
                        [mm[1]])
                    n_q0k = g.add("vector", lambda E: E.tensor_scalar(
                        xn_scr2[:, 0:512], psb[2], 5.0, -5.0, ALU.min, ALU.max),
                        [mm[2], prev.get("xn2_rd")])
                    n_q1k = g.add("vector", lambda E: E.tensor_scalar(
                        xn_scr2[:, 512:1024], psb[3], 5.0, -5.0, ALU.min, ALU.max),
                        [mm[3]])
                    prev[("bank_rd", 0)] = n_q0q
                    prev[("bank_rd", 1)] = n_q1q
                    prev[("bank_rd", 2)] = n_q0k
                    prev[("bank_rd", 3)] = n_q1k
                    va = v_aug[:, t0v(i):t0v(i) + NHEAD * HW].rearrange(
                        "p (h m) -> p h m", m=HW)
                    n_v1 = g.add("vector", lambda E, va=va: E.tensor_copy(
                        va[:, 0:8, 0:64],
                        psb[4].rearrange("p (h k) -> p h k", h=8)), [mm[4]])
                    n_v2 = g.add("vector", lambda E, va=va: E.tensor_copy(
                        va[:, 8:16, 0:64],
                        psb[5].rearrange("p (h k) -> p h k", h=8)), [mm[5]])
                    prev[("bank_rd", 4)] = n_v1
                    prev[("bank_rd", 5)] = n_v2
                    pend_tail = (i, b, b3, n_ev, n_q0q, n_q1q, n_q0k, n_q1k)
                pcq, pck = emit_projh(*pend_tail)
                emit_qk_tr(pend_tail[0], pend_tail[2], pend_tail[1], pcq, pck)
                g.lower()
                exw.pop_all().close()

            with ExitStack() as exw:
                woT = exw.enter_context(
                    nc.sbuf_tensor(f"a0_wo{s}", [128, NC * 1024], BF16,
                                   side="right"))[:]
                with ExitStack() as exw2:
                    sbw2 = lambda nm, sh, dt: exw2.enter_context(
                        nc.sbuf_tensor(nm, sh, dt, side="right"))[:]
                    stage = [sbw2(f"a0_so{s}{i}", [128, 1024], BF16)
                             for i in range(2)]
                    ps_wg = exw2.enter_context(
                        nc.psum_tensor(f"a0_pwg{s}", [128, 1024], BF16))[:]
                    weight_graph([woT], [wo_ap], stage, ps_wg)
                ps_s2 = [exw.enter_context(
                    nc.psum_tensor(f"a0_ps2{s}{i}", [128, 1024], F32))[:]
                    for i in range(2)]
                ps_pv = exw.enter_context(
                    nc.psum_tensor(f"a0_ppv{s}", [128, 1024], F32))[:]
                ps_wo = exw.enter_context(
                    nc.psum_tensor(f"a0_pwo{s}", [128, 1024], F32))[:]
                pT2 = [exw.enter_context(
                    nc.sbuf_tensor(f"a0_pT2{s}{i}", [128, 1024], BF16,
                                   side="right"))[:] for i in range(4)]
                att2 = [exw.enter_context(
                    nc.sbuf_tensor(f"a0_at2{s}{i}", [128, NC * 512], BF16,
                                   side="right"))[:] for i in range(2)]

                g = Graph(nc, sems, counts)
                prev = {}
                wo_queue = []

                def emit_pair(hp, qc, pending_norm):
                    att_c = att2[qc % 2]
                    cT = hp * T
                    pvs = [None, None]
                    exps = {}
                    for kt in range(NTS):
                        pb = kt % 2
                        pb4 = kt % 4
                        sb_ = ps_s2[pb]
                        n_mmA = g.add("tensor", lambda E, cT=cT, kt=kt, qc=qc,
                                      sb_=sb_: E.matmul(
                            sb_[:, 0:512],
                            k_fm[0:64, cT + kt * 128:cT + (kt + 1) * 128],
                            q_fm[0:64, cT + qc * 512:cT + (qc + 1) * 512],
                            start=True, stop=True, tile_position=(0, 0)),
                            [prev.get(("exp", pb))])
                        n_mmB = g.add("tensor", lambda E, cT=cT, kt=kt, qc=qc,
                                      sb_=sb_: E.matmul(
                            sb_[:, 512:1024],
                            k_fm[64:128, cT + kt * 128:cT + (kt + 1) * 128],
                            q_fm[64:128, cT + qc * 512:cT + (qc + 1) * 512],
                            start=True, stop=True, tile_position=(64, 0)), [])
                        n_ex = g.add("scalar", lambda E, pb4=pb4, sb_=sb_:
                                     E.activation(pT2[pb4], sb_, AF.Exp,
                                                  scale=0.125),
                            [n_mmB, prev.get(("pT_rd", pb4))])
                        prev[("exp", pb)] = n_ex
                        exps[kt] = n_ex
                        if kt == 2 and pending_norm is not None:
                            emit_norm(*pending_norm)
                            pending_norm = None
                        if kt >= 3:
                            kp = kt - 3
                            pvs = emit_pv(hp, kp, exps[kp], pvs)
                    for kp in (NTS - 3, NTS - 2, NTS - 1):
                        pvs = emit_pv(hp, kp, exps[kp], pvs)
                    return (hp, qc, pvs)

                def emit_norm(hp, qc, pvs):
                    att_c = att2[qc % 2]
                    n_rlA = g.add("vector", lambda E: E.reciprocal(
                        rl_row[:, 0:512], ps_pv[64:65, 0:512]),
                        [pvs[0], prev.get("rbc_done")])
                    n_rlB = g.add("vector", lambda E: E.reciprocal(
                        rl_row[:, 512:1024], ps_pv[64:65, 512:1024]),
                        [pvs[1]])
                    n_rbcA = g.add("tensor", lambda E: E.matmul(
                        ps_pv[64:128, 0:512], ones_row[:, 0:64],
                        rl_row[:, 0:512], start=True, stop=True), [n_rlA])
                    n_rbcB = g.add("tensor", lambda E: E.matmul(
                        ps_pv[64:128, 512:1024], ones_row[:, 0:64],
                        rl_row[:, 512:1024], start=True, stop=True), [n_rlB])
                    prev["rbc_done"] = n_rbcB
                    n_rbeA = g.add("vector", lambda E: E.tensor_copy(
                        rb_sb[0:64, 0:512], ps_pv[64:128, 0:512]), [n_rbcA])
                    n_rbeB = g.add("vector", lambda E: E.tensor_copy(
                        rb_sb[64:128, 0:512], ps_pv[64:128, 512:1024]), [n_rbcB])
                    n_attA = g.add("vector", lambda E, att_c=att_c, hp=hp:
                                   E.tensor_mul(
                        att_c[0:64, hp * 512:(hp + 1) * 512],
                        ps_pv[0:64, 0:512], rb_sb[0:64, 0:512]),
                        [n_rbeA, prev.get(("att_rd", qc % 2))])
                    n_attB = g.add("vector", lambda E, att_c=att_c, hp=hp:
                                   E.tensor_mul(
                        att_c[64:128, hp * 512:(hp + 1) * 512],
                        ps_pv[0:64, 512:1024], rb_sb[64:128, 0:512]),
                        [n_rbeB])
                    prev["pv_free"] = (n_attA, n_attB)
                    return n_attB

                def emit_pv(hp, kt, n_ex, pvs):
                    for hb in range(2):
                        dd = [n_ex]
                        if kt == 0:
                            dd += list(prev.get("pv_free", ()))
                        pvs[hb] = g.add("tensor", lambda E, hp=hp, kt=kt, hb=hb:
                                        E.matmul(
                            ps_pv[0:HW, hb * 512:(hb + 1) * 512],
                            v_aug[:, t0v(kt) + (hp * 2 + hb) * HW:
                                  t0v(kt) + (hp * 2 + hb) * HW + HW],
                            pT2[kt % 4][:, hb * 512:(hb + 1) * 512],
                            start=(kt == 0), stop=(kt == NTS - 1)), dd)
                    prev[("pT_rd", kt % 4)] = pvs[1]
                    return pvs

                def emit_wo_tile(qc, tt):
                    t = s * NTS + qc * 4 + tt
                    b = tt % 2
                    att_c = att2[qc % 2]
                    mmw = [None, None]
                    for c in range(NC):
                        for h2 in range(2):
                            dep0 = []
                            if c == 0:
                                dep0 = [prev.get(("att_done", qc % 2)),
                                        prev.get(("oev", b)),
                                        prev.get("wo_bank")]
                            mmw[h2] = g.add("tensor", lambda E, c=c, h2=h2, tt=tt,
                                            att_c=att_c: E.matmul(
                                ps_wo[:, h2 * 512:(h2 + 1) * 512],
                                att_c[:, c * 512 + tt * 128:c * 512 + (tt + 1) * 128],
                                woT[:, c * 1024 + h2 * 512:c * 1024 + (h2 + 1) * 512],
                                start=(c == 0), stop=(c == NC - 1)), dep0)
                    prev[("att_rd", qc % 2)] = mmw[1]
                    n_oe = g.add("scalar", lambda E, b=b: E.activation(
                        buf_a[b][:, 0:512], ps_wo[:, 0:512], AF.Copy),
                        [mmw[0], prev.get(("oa_rd", b))])
                    n_oe2 = g.add("scalar", lambda E, b=b: E.activation(
                        buf_a[b][:, 512:1024], ps_wo[:, 512:1024], AF.Copy),
                        [mmw[1], prev.get(("oa_rd", b))])
                    prev["wo_bank"] = n_oe2
                    n_vs_ = g.add("scalar", lambda E, b=b: E.activation(
                        sqdump[:, 1:1024], buf_a[b][:, 1:1024], AF.Square),
                        [n_oe, n_oe2, prev.get("sq_war"), prev.get(("sm_rd", b))])
                    n_vs = g.add("vector", lambda E, b=b: E.reduce_sum(
                        sm[b][:, 0:1], sqdump[:, 1:1024], axis=AX.X), [n_vs_])
                    prev["sq_war"] = n_vs
                    n_v1 = g.add("scalar", lambda E, b=b: E.activation(
                        sm[b][:, 1:2], sm[b][:, 0:1], AF.Ln, bias=CH2), [n_vs])
                    n_v2 = g.add("scalar", lambda E, b=b: E.activation(
                        sm[b][:, 2:3], sm[b][:, 1:2], AF.Exp, scale=0.5), [n_v1])
                    n_vn = g.add("vector", lambda E, b=b: E.tensor_single_scalar(
                        sm[b][:, 3:4], sm[b][:, 2:3], 5.0, ALU.min), [n_v2])
                    n_e1 = g.add("scalar", lambda E, b=b: E.activation(
                        sm[b][:, 4:5], sm[b][:, 3:4], AF.Exp), [n_vn])
                    n_e2 = g.add("scalar", lambda E, b=b: E.activation(
                        sm[b][:, 5:6], sm[b][:, 3:4], AF.Exp, scale=-1.0), [n_e1])
                    n_sh = g.add("vector", lambda E, b=b: E.tensor_sub(
                        sm[b][:, 6:7], sm[b][:, 4:5], sm[b][:, 5:6]), [n_e2])
                    n_de = g.add("vector", lambda E, b=b: E.tensor_single_scalar(
                        sm[b][:, 7:8], sm[b][:, 3:4], CH2, ALU.add), [n_vn])
                    n_rc = g.add("vector", lambda E, b=b: E.reciprocal(
                        sm[b][:, 8:9], sm[b][:, 7:8]), [n_de])
                    n_sc = g.add("vector", lambda E, b=b: E.tensor_scalar(
                        sm[b][:, 9:10], sm[b][:, 6:7], sm[b][:, 8:9], 0.5,
                        ALU.mult, ALU.mult), [n_sh, n_rc])
                    n_s1 = g.add("vector", lambda E, b=b: E.tensor_scalar(
                        spt[:, 1:1024], buf_a[b][:, 1:1024], sm[b][:, 9:10], 8.0,
                        ALU.mult, ALU.min), [n_sc, prev.get("sp_rd")])
                    n_s2 = g.add("vector", lambda E: E.tensor_scalar_max(
                        spt[:, 1:1024], spt[:, 1:1024], -8.0), [n_s1])
                    n_q2_ = g.add("scalar", lambda E, b=b: E.activation(
                        sqdump[:, 1:1024], spt[:, 1:1024], AF.Square),
                        [n_s2, prev["sq_war"]])
                    n_q2 = g.add("vector", lambda E, b=b: E.reduce_sum(
                        sm[b][:, 10:11], sqdump[:, 1:1024], axis=AX.X), [n_q2_])
                    prev["sq_war"] = n_q2
                    n_x1 = g.add("scalar", lambda E, b=b: E.activation(
                        sm[b][:, 11:12], sm[b][:, 10:11], AF.Ln, bias=CH1), [n_q2])
                    n_x2 = g.add("scalar", lambda E, b=b: E.activation(
                        sm[b][:, 12:13], sm[b][:, 11:12], AF.Exp, scale=0.5),
                        [n_x1])
                    n_lx = g.dma("scalar", xn_scr, xin_t[:, t, :],
                                 [prev.get("xn_rd2")])
                    n_r1 = g.add("vector", lambda E, b=b: E.tensor_add(
                        buf_a[b][:, 1:1024], xn_scr[:, 1:1024], spt[:, 1:1024]),
                        [n_s2, n_lx, n_oe2, n_vs])
                    prev["sp_rd"] = n_r1
                    n_r0 = g.add("vector", lambda E, b=b: E.tensor_add(
                        buf_a[b][:, 0:1], xn_scr[:, 0:1], sm[b][:, 12:13]),
                        [n_x2, n_lx])
                    prev[("sm_rd", b)] = n_r0
                    prev["xn_rd2"] = n_r0
                    n_st = g.dma("gpsimd", xout_t[:, t, :], buf_a[b],
                                 [n_r1, n_r0])
                    prev[("oa_rd", b)] = n_st
                    prev[("oev", b)] = n_st

                import os as _os
                _skip_wo = _os.environ.get("A0_SKIP_WO") == "1"
                _no_ilv = _os.environ.get("A0_ILV") != "1"
                pending_norm = None
                for qc in range(4):
                    for hp in range(NC):
                        pending_norm = emit_pair(hp, qc, pending_norm)
                        if not _skip_wo and not _no_ilv and wo_queue and hp >= 4:
                            emit_wo_tile(*wo_queue.pop(0))
                    att_done = emit_norm(*pending_norm)
                    pending_norm = None
                    prev[("att_done", qc % 2)] = att_done
                    if _skip_wo:
                        prev[("att_rd", qc % 2)] = att_done
                        prev[("oa_rd", qc % 4 % 2)] = g.dma(
                            "sync", xout_t[:, s * NTS + qc * 4, :],
                            buf_a[0], [att_done])
                    else:
                        wo_queue.extend((qc, tt) for tt in range(4))
                while wo_queue:
                    emit_wo_tile(*wo_queue.pop(0))
                g.add("sync", lambda E: E.nop(),
                      [prev[("oa_rd", 0)], prev[("oa_rd", 1)]])
                g.lower()


def t0v(i):
    return i * NHEAD * 65


def emit_l1attn(nc, sems, counts, xin, xout, wqkv_ap, dbg=None):
    xin_t = xin.rearrange("(t p) d -> p t d", p=128)
    xout_t = xout.rearrange("(t p) d -> p t d", p=128)
    SCALE = 1.0 / (D ** 0.5)
    with ExitStack() as ex:
        sb = lambda nm, sh, dt: ex.enter_context(nc.sbuf_tensor(nm, sh, dt))[:]
        psa = lambda nm, sh, dt=F32: ex.enter_context(nc.psum_tensor(nm, sh, dt))[:]

        wqkv = sb("a1_w", [128, NC * 3072], BF16)
        xn_fm = sb("a1_xf", [128, NC * T], BF16)
        q_fm = sb("a1_q", [128, NC * T], BF16)
        k_fm = sb("a1_k", [128, NC * T], BF16)
        v_tm = sb("a1_v", [128, NTS * 1024], BF16)
        ident_bf = sb("a1_id", [128, 128], BF16)
        tri = sb("a1_tri", [128, 128], F32)
        cpool = sb("a1_cp", [128, 1], F32)
        stats = sb("a1_st", [128, 16 * 6], F32)
        S = lambda k: stats[:, 16 * k:16 * (k + 1)]
        sqdump = sb("a1_sqd", [128, 1024], F32)
        xt = [sb(f"a1_xt{i}", [128, 1024], F32) for i in range(2)]
        xnbf = [sb(f"a1_xnb{i}", [128, 1024], BF16) for i in range(2)]
        p_sb = sb("a1_p", [128, 2048], BF16)
        pt_sb = [sb(f"a1_pt{i}", [128, 128], BF16) for i in range(2)]
        lrow = sb("a1_l", [128, 8], F32)
        o_sb = [sb(f"a1_o{i}", [128, 1024], F32) for i in range(2)]
        ps_tr = psa("a1_ptr", [128, 1024], BF16)
        ps_s = psa("a1_ps", [128, 2048])
        ps_pt = psa("a1_ppt", [128, 128], BF16)
        ps_o = psa("a1_po", [128, 1024])

        g = Graph(nc, sems, counts)
        emit_consts(g, dict(ident_bf=ident_bf))
        n0 = g.add("gpsimd", lambda E: E.memset(tri, 0.0))
        g.add("gpsimd", lambda E: E.affine_select(
            out=tri, in_=tri, compare_op=ALU.is_ge, fill=-1e30, base=0,
            pattern=[[-1, 128]], channel_multiplier=1), deps=[n0])
        g.add("gpsimd", lambda E: E.memset(cpool[:, 0:1], EPS_LN))
        n_wqkv = g.dma("gpsimd", wqkv.rearrange("p (c e) -> p c e", c=NC),
                       wqkv_ap.rearrange("(c p) e -> p c e", p=128))
        g.lower(drain_gpsimd=False)

        for s in range(SEQ_PER_CORE):
            g = Graph(nc, sems, counts)
            n_prev_sq = None
            prev_ld = {}
            for i in range(NTS):
                t = s * NTS + i
                b = i % 2
                n_l = g.dma("sync", xt[b], xin_t[:, t, :], [prev_ld.get(b)])
                _, n_prev_sq = ln_stats_nodes(
                    g, xt[b], S(0)[:, i:i + 1], S(1)[:, i:i + 1], sqdump,
                    [n_l], sq_war=n_prev_sq)
                prev_ld[b] = n_prev_sq
            n_mu, n_rstd = ln_chain_nodes(
                g, S(0), S(1), S(2), S(3), S(4), S(5),
                [n_prev_sq, prev_ld[0], prev_ld[1]], cpool[:, 0:1])
            g.lower()

            g = Graph(nc, sems, counts)
            prev = {}
            for i in range(NTS):
                t = s * NTS + i
                b = i % 2
                n_l = g.dma("sync", xt[b], xin_t[:, t, :], [prev.get(("xt_rd", b))])
                n_xn = g.add("vector", lambda E, b=b, i=i: E.tensor_scalar(
                    xnbf[b], xt[b], S(2)[:, i:i + 1], S(3)[:, i:i + 1],
                    ALU.subtract, ALU.mult), [n_l, prev.get(("xnbf_rd", b))])
                prev[("xt_rd", b)] = n_xn
                n_ev = transpose_tile_nodes(
                    g, ps_tr, xnbf[b],
                    xn_fm.rearrange("p (c t2) -> p c t2",
                                    c=NC)[:, :, i * 128:(i + 1) * 128],
                    ident_bf, [n_xn], evac_eng="scalar",
                    bank_free=prev.get("tr_bank"))
                prev[("xnbf_rd", b)] = n_ev
                prev["tr_bank"] = n_ev
            n_xnfm = n_ev
            last_evac = None
            for which, dst in ((0, q_fm), (1, k_fm)):
                for o in range(NC):
                    mm = None
                    for c in range(NC):
                        for tc in range(4):
                            dep0 = []
                            if c == 0 and tc == 0:
                                dep0 = [n_xnfm, last_evac, n_wqkv]
                            mm = g.add("tensor", lambda E, which=which, o=o, c=c, tc=tc:
                                       E.matmul(
                                ps_s[:, tc * 512:(tc + 1) * 512],
                                wqkv[:, c * 3072 + which * 1024 + o * 128:
                                     c * 3072 + which * 1024 + (o + 1) * 128],
                                xn_fm[:, c * T + tc * 512: c * T + (tc + 1) * 512],
                                start=(c == 0), stop=(c == NC - 1)), dep0)
                    last_evac = g.add("scalar", lambda E, dst=dst, o=o: E.copy(
                        dst[:, o * T:(o + 1) * T], ps_s[:, 0:2048]), [mm])
            for i in range(NTS):
                mm2 = [None, None]
                for c in range(NC):
                    for h in range(2):
                        dep0 = []
                        if c == 0 and h == 0:
                            dep0 = [n_xnfm, prev.get("v_evac") or last_evac, n_wqkv]
                        mm2[h] = g.add("tensor", lambda E, i=i, c=c, h=h: E.matmul(
                            ps_o[:, h * 512:(h + 1) * 512],
                            xn_fm[:, c * T + i * 128: c * T + (i + 1) * 128],
                            wqkv[:, c * 3072 + 2048 + h * 512:
                                 c * 3072 + 2048 + (h + 1) * 512],
                            start=(c == 0), stop=(c == NC - 1)), dep0)
                prev["v_evac"] = g.add("scalar", lambda E, i=i: E.copy(
                    v_tm[:, i * 1024:(i + 1) * 1024], ps_o[:, 0:1024]),
                    [mm2[0], mm2[1]])
            g.lower()

            g = Graph(nc, sems, counts)
            prev = {}
            for i in range(NTS):
                t = s * NTS + i
                b = i % 2
                kspan = (i + 1) * 128
                nk5 = (kspan + 511) // 512
                mm = None
                for c in range(NC):
                    for kc in range(nk5):
                        w = min(512, kspan - kc * 512)
                        dep0 = []
                        if c == 0 and kc == 0:
                            dep0 = [prev.get("exp_done"), prev.get("mask_done")]
                        mm = g.add("tensor", lambda E, i=i, c=c, kc=kc, w=w: E.matmul(
                            ps_s[:, kc * 512: kc * 512 + w],
                            q_fm[:, c * T + i * 128:c * T + (i + 1) * 128],
                            k_fm[:, c * T + kc * 512: c * T + kc * 512 + w],
                            start=(c == 0), stop=(c == NC - 1)), dep0)
                n_mask = g.add("vector", lambda E, i=i: E.tensor_add(
                    ps_s[:, i * 128:(i + 1) * 128],
                    ps_s[:, i * 128:(i + 1) * 128], tri), [mm])
                prev["mask_done"] = n_mask
                n_mx = g.add("vector", lambda E, kspan=kspan: E.reduce_max(
                    lrow[:, 4:5], ps_s[:, 0:kspan], axis=AX.X),
                    [n_mask, prev.get("exp_done")])
                n_mg = g.add("vector", lambda E: E.tensor_scalar_mul(
                    lrow[:, 5:6], lrow[:, 4:5], -SCALE), [n_mx])
                n_exp = None
                for kc in range(nk5):
                    w = min(512, kspan - kc * 512)
                    n_exp = g.add("scalar", lambda E, kc=kc, w=w: E.activation(
                        p_sb[:, kc * 512:kc * 512 + w], ps_s[:, kc * 512:kc * 512 + w],
                        AF.Exp, bias=lrow[:, 5:6], scale=SCALE),
                        [n_mg] + ([prev.get("pv_done"), prev.get("l_done")]
                                  if kc == 0 else []))
                prev["exp_done"] = n_exp
                n_l = g.add("vector", lambda E, kspan=kspan: E.reduce_sum(
                    lrow[:, 6:7], p_sb[:, 0:kspan], axis=AX.X), [n_exp])
                n_rl = g.add("vector", lambda E: E.reciprocal(
                    lrow[:, 7:8], lrow[:, 6:7]), [n_l])
                prev["l_done"] = n_rl
                pv_last = None
                for kt in range(i + 1):
                    pb = kt % 2
                    n_tr = g.add("tensor", lambda E, kt=kt: E.transpose(
                        ps_pt, p_sb[:, kt * 128:(kt + 1) * 128], ident_bf),
                        [n_exp, prev.get("pt_evac")])
                    n_pte = g.add("scalar", lambda E, pb=pb: E.copy(
                        pt_sb[pb], ps_pt), [n_tr, prev.get(("pt_rd", pb))])
                    prev["pt_evac"] = n_pte
                    for h in range(2):
                        pv_last = g.add("tensor", lambda E, kt=kt, h=h, pb=pb, i=i:
                                        E.matmul(
                            ps_o[:, h * 512:(h + 1) * 512], pt_sb[pb],
                            v_tm[:, kt * 1024 + h * 512: kt * 1024 + (h + 1) * 512],
                            start=(kt == 0), stop=(kt == i)),
                            [n_pte, prev.get("o_evac")] if kt == 0 and h == 0 else [n_pte])
                    prev[("pt_rd", pb)] = pv_last
                prev["pv_done"] = pv_last
                n_lx = g.dma("sync", xt[b], xin_t[:, t, :], [prev.get(("xt_rd", b))])
                n_o1 = g.add("vector", lambda E, b=b: E.tensor_scalar_mul(
                    o_sb[b], ps_o[:, 0:1024], lrow[:, 7:8]),
                    [pv_last, n_rl, prev.get(("o_rd", b))])
                prev["o_evac"] = n_o1
                n_o2 = g.add("vector", lambda E, b=b: E.tensor_add(
                    o_sb[b], o_sb[b], xt[b]), [n_o1, n_lx])
                prev[("xt_rd", b)] = n_o2
                n_st = g.dma("gpsimd", xout_t[:, t, :], o_sb[b], [n_o2])
                prev[("o_rd", b)] = n_st
            g.add("sync", lambda E: E.nop(),
                  [prev[("o_rd", 0)], prev[("o_rd", 1)]])
            g.lower()


def emit_l1ffn(nc, sems, counts, xin, xout, qw_ap, qs_ap, qb_ap, la_ap, lb_ap,
               lg_ap, lbeta_ap, scratch):
    xin_t = xin.rearrange("(t p) d -> p t d", p=128)
    xout_t = xout.rearrange("(t p) d -> p t d", p=128)
    hs = [scratch[k].rearrange("(t p) d -> p t d", p=128)
          for k in ("h0", "h1", "h2")]
    y1_t = scratch["y1"].rearrange("(t p) d -> p t d", p=128)
    with ExitStack() as ex:
        sb = lambda nm, sh, dt: ex.enter_context(nc.sbuf_tensor(nm, sh, dt))[:]
        psa = lambda nm, sh, dt=F32: ex.enter_context(nc.psum_tensor(nm, sh, dt))[:]

        ident_bf = sb("f1_id", [128, 128], BF16)
        ones_k1 = sb("f1_on1", [1, 128], BF16)
        cpool = sb("f1_cp", [128, 1], F32)
        wdqT = [sb(f"f1_wd{j}", [128, NC * 1024], BF16) for j in range(2)]
        laT = [sb(f"f1_la{j}", [128, 8 * 32], BF16) for j in range(6)]
        lbT = [sb(f"f1_lb{j}", [32, 1024], BF16) for j in range(6)]
        bias_b = [sb(f"f1_bb{j}", [1, 1024], BF16) for j in range(6)]
        gam_b = sb("f1_gb", [128, 1024], F32)
        bet_b = sb("f1_be", [128, 1024], F32)
        la_tmp = sb("f1_lat", [32, 1024], BF16)
        lb_tmp = sb("f1_lbt", [128, 8 * 32], BF16)
        q_i32 = [sb(f"f1_qi{j}", [128, 1024], I32) for j in range(2)]
        w_f32 = sb("f1_wf", [128, 1024], F32)
        w_bf = [sb(f"f1_wb{j}", [128, 1024], BF16) for j in range(2)]
        scl_sb = sb("f1_sc", [128, 8 * 64], F32)
        stats = sb("f1_st", [128, 64 * 2 + 32 * 6], F32)
        S1 = stats[:, 0:64]
        S2 = stats[:, 64:128]
        MUA = stats[:, 128:160]
        RSA = stats[:, 160:192]
        SCR0 = stats[:, 192:224]
        SCR1 = stats[:, 224:256]
        sqdump = [sb(f"f1_sqd{i}", [128, 1024], F32) for i in range(2)]
        ht = [sb(f"f1_ht{i}", [128, 1024], F32) for i in range(4)]
        hbf = [sb(f"f1_hb{i}", [128, 1024], BF16) for i in range(2)]
        hf = [sb(f"f1_hf{i}", [128, 1024], BF16) for i in range(4)]
        QT = 16
        y_big = sb("f1_yb", [128, QT * 1024], F32)
        g_sb = [sb(f"f1_g{i}", [128, 1024], F32) for i in range(2)]
        tT_bf = [sb(f"f1_tt{i}", [32, 128], BF16) for i in range(4)]
        out_sb = [sb(f"f1_o{i}", [128, 1024], F32) for i in range(2)]
        ps_tr = psa("f1_ptr", [128, 1024], BF16)
        ps_y = [psa(f"f1_py{i}", [128, 512]) for i in range(4)]
        ps_lt = [psa(f"f1_plt{i}", [128, 512]) for i in range(2)]
        ps_tr2 = psa("f1_ptr2", [128, 1024], BF16)

        n_bias = [None] * 6
        g = Graph(nc, sems, counts)
        emit_consts(g, dict(ident_bf=ident_bf))
        g.add("gpsimd", lambda E: E.memset(ones_k1, 1.0))
        g.add("gpsimd", lambda E: E.memset(cpool[:, 0:1], EPS_LN))
        prev_ev = None
        for j in range(6):
            n_la = g.dma("gpsimd", la_tmp[0:32, :], la_ap[j], [prev_ev])
            trs = []
            for c in range(NC):
                trs.append(g.add("tensor", lambda E, c=c: E.transpose(
                    ps_tr[:, c * 32:(c + 1) * 32], la_tmp[0:32, c * 128:(c + 1) * 128],
                    ident_bf[0:32, 0:32]),
                    [n_la, prev_ev] if c == 0 else []))
            n_ev1 = g.add("scalar", lambda E, j=j: E.copy(
                laT[j], ps_tr[:, 0:256]), [trs[-1]])
            n_lb = g.dma("gpsimd", lb_tmp.rearrange("p (c r) -> p c r", c=NC),
                         lb_ap[j].rearrange("(c p) r -> p c r", p=128), [n_ev1])
            trs = []
            for c in range(NC):
                trs.append(g.add("tensor", lambda E, c=c: E.transpose(
                    ps_tr[0:32, c * 128:(c + 1) * 128], lb_tmp[:, c * 32:(c + 1) * 32],
                    ident_bf), [n_lb, n_ev1] if c == 0 else []))
            n_ev2 = g.add("scalar", lambda E, j=j: E.copy(
                lbT[j], ps_tr[0:32, 0:1024]), [trs[-1]])
            prev_ev = n_ev2
            n_bias[j] = g.dma("gpsimd", bias_b[j], qb_ap[j].unsqueeze(0))
        g.lower(drain_gpsimd=False)

        def dequant_graph(blk):
            g = Graph(nc, sems, counts)
            qrd = {}
            prev_ev = None
            prev_scl = None
            for j in range(2):
                idx = 2 * blk + j
                n_sc = g.dma("sync", scl_sb.rearrange("p (r g2) -> p r g2", r=NC),
                             qs_ap[idx].rearrange("(r p) g2 -> p r g2", p=128),
                             [prev_scl])
                for r in range(NC):
                    qb_ = (j * NC + r) % 2
                    n_q = g.dma("gpsimd", q_i32[qb_],
                                qw_ap[idx, r * 128:(r + 1) * 128, :],
                                [qrd.get(qb_)])
                    n_cvt = g.add("vector", lambda E, qb_=qb_: E.tensor_scalar(
                        w_f32, q_i32[qb_], 1.0 / 7.5, -1.0, ALU.mult, ALU.add),
                        [n_q, prev_scl])
                    qrd[qb_] = n_cvt
                    n_scl = g.add("vector", lambda E, r=r, j=j: E.tensor_tensor(
                        w_bf[j].rearrange("p (g2 k) -> p g2 k", g2=64),
                        w_f32.rearrange("p (g2 k) -> p g2 k", g2=64),
                        scl_sb[:, r * 64:(r + 1) * 64].unsqueeze(2).broadcast_to(
                            [128, 64, 16]),
                        ALU.mult), [n_cvt, n_sc, prev_ev])
                    prev_scl = n_scl
                    n_ev = transpose_tile_nodes(
                        g, ps_tr, w_bf[j],
                        wdqT[j].rearrange("p (c o) -> p c o",
                                          c=NC)[:, :, r * 128:(r + 1) * 128],
                        ident_bf, [n_scl], evac_eng="scalar",
                        bank_free=prev_ev)
                    prev_ev = n_ev
            return g

        g = Graph(nc, sems, counts)
        n_prev_sq = None
        prev_ld = {}
        for t in range(NT):
            b = t % 2
            n_l = g.dma("sync", ht[b], xin_t[:, t, :], [prev_ld.get(b)])
            _, n_prev_sq = ln_stats_nodes(
                g, ht[b], S1[:, t:t + 1], S2[:, t:t + 1], sqdump[b], [n_l],
                sq_war=n_prev_sq)
            prev_ld[b] = n_prev_sq
        n_mu, n_rstd = ln_chain_nodes(
            g, S1[:, 0:32], S2[:, 0:32], MUA, RSA, SCR0, SCR1,
            [n_prev_sq, prev_ld[0], prev_ld[1]], cpool[:, 0:1])
        g.lower()
        g = Graph(nc, sems, counts)
        prev = {}
        for t in range(NT):
            b = t % 2
            n_l = g.dma("sync", ht[b], xin_t[:, t, :], [prev.get(("rd", b))])
            n_xn = g.add("vector", lambda E, b=b, t=t: E.tensor_scalar(
                out_sb[b], ht[b], MUA[:, t:t + 1], RSA[:, t:t + 1],
                ALU.subtract, ALU.mult), [n_l, prev.get(("st", b))])
            prev[("rd", b)] = n_xn
            n_st = g.dma("gpsimd", hs[0][:, t, :], out_sb[b], [n_xn])
            prev[("st", b)] = n_st
        g.add("sync", lambda E: E.nop(), [prev[("st", 0)], prev[("st", 1)]])
        g.lower()

        for blk in range(3):
            dequant_graph(blk).lower()
            h_in = hs[blk]
            i0, i1 = 2 * blk, 2 * blk + 1
            last = blk == 2
            h_out = xout_t if last else hs[blk + 1]

            g = Graph(nc, sems, counts)
            n_gam = g.dma("sync", gam_b,
                          lg_ap[blk].unsqueeze(0).partition_broadcast(128).squeeze(1))
            n_bet = g.dma("sync", bet_b,
                          lbeta_ap[blk].unsqueeze(0).partition_broadcast(128).squeeze(1))
            g.lower()

            for sh in range(NT // QT):
             t0, t1_ = sh * QT, (sh + 1) * QT
             g = Graph(nc, sems, counts)
             prev = {}

             def a_tail(t, i, b, b2, mm0, mm1):
                yb_i = y_big[:, i * 1024:(i + 1) * 1024]
                n_e = [None, None]
                for h, mmh in ((0, mm0), (1, mm1)):
                    n_e[h] = g.add("scalar", lambda E, h=h, i=i, b2=b2:
                                   E.activation(
                        y_big[:, i * 1024 + h * 512:i * 1024 + (h + 1) * 512],
                        ps_y[b2 * 2 + h], AF.Copy), [mmh])
                prev[("y_ev", b2)] = n_e[1]
                n_r1_ = g.add("vector", lambda E, yb_i=yb_i, t=t: E.reduce_sum(
                    S1[:, t:t + 1], yb_i, axis=AX.X), [n_e[0], n_e[1]])
                n_sqd = g.add("scalar", lambda E, yb_i=yb_i, b2=b2: E.activation(
                    sqdump[b2], yb_i, AF.Square),
                    [n_e[0], n_e[1], prev.get(("sqd_rd", b2))])
                n_r2_ = g.add("vector", lambda E, b2=b2, t=t: E.reduce_sum(
                    S2[:, t:t + 1], sqdump[b2], axis=AX.X), [n_sqd])
                prev[("sqd_rd", b2)] = n_r2_
                return n_r1_, n_r2_

             def a_mms(t, i, b, b2, n_ev):
                n_lt = None
                for c in range(NC):
                    n_lt = g.add("tensor", lambda E, c=c, b=b, b2=b2, i0=i0: E.matmul(
                        ps_lt[b2][0:32, 0:128], laT[i0][:, c * 32:(c + 1) * 32],
                        hf[b][:, c * 128:(c + 1) * 128],
                        start=(c == 0), stop=(c == NC - 1)),
                        [n_ev, prev.get(("lt_ev", b2))] if c == 0 else [])
                n_lte = g.add("scalar", lambda E, b=b, b2=b2: E.copy(
                    tT_bf[b], ps_lt[b2][0:32, 0:128]),
                    [n_lt, prev.get(("tT_rd", b))])
                prev[("lt_ev", b2)] = n_lte
                mm_last = [None, None]
                for c in range(NC):
                    for h in range(2):
                        dep0 = [n_ev, prev.get(("y_ev", b2))] if c == 0 else []
                        mm_last[h] = g.add("tensor", lambda E, c=c, h=h, b=b, b2=b2, j=0:
                                           E.matmul(
                            ps_y[b2 * 2 + h], hf[b][:, c * 128:(c + 1) * 128],
                            wdqT[j][:, c * 1024 + h * 512: c * 1024 + (h + 1) * 512],
                            start=(c == 0), stop=False), dep0)
                prev[("hf_rd", b)] = mm_last[1]
                for h in range(2):
                    mm_last[h] = g.add("tensor", lambda E, h=h, i0=i0, b=b, b2=b2: E.matmul(
                        ps_y[b2 * 2 + h], ones_k1, bias_b[i0][:, h * 512:(h + 1) * 512],
                        start=False, stop=False), [n_bias[i0]] if h == 0 else [])
                for h in range(2):
                    mm_last[h] = g.add("tensor", lambda E, h=h, i0=i0, b=b, b2=b2: E.matmul(
                        ps_y[b2 * 2 + h], tT_bf[b], lbT[i0][0:32, h * 512:(h + 1) * 512],
                        start=False, stop=True), [n_lte] if h == 0 else [])
                prev[("tT_rd", b)] = mm_last[1]
                return (t, i, b, b2, mm_last[0], mm_last[1])

             pend_mm = None
             pend_tail = None
             for t in range(t0, t1_):
                b = t % 4
                b2 = t % 2
                i = t - t0
                n_l = g.dma("sync", ht[b], h_in[:, t, :], [prev.get(("ht_rd", b))])
                n_cb = g.add("vector", lambda E, b=b, b2=b2: E.tensor_copy(
                    hbf[b2], ht[b]), [n_l, prev.get(("hbf_rd", b2))])
                prev[("ht_rd", b)] = n_cb
                n_ev = transpose_tile_nodes(
                    g, [ps_tr, ps_tr2][b2], hbf[b2], hf[b], ident_bf, [n_cb],
                    evac_eng="scalar", bank_free=prev.get(("tr_bank", b2)),
                    evac_deps=[prev.get(("hf_rd", b))])
                prev[("hbf_rd", b2)] = n_ev
                prev[("tr_bank", b2)] = n_ev
                if pend_mm is not None:
                    new_tail = a_mms(*pend_mm)
                    if pend_tail is not None:
                        r12 = a_tail(*pend_tail)
                    pend_tail = new_tail
                pend_mm = (t, i, b, b2, n_ev)
             new_tail = a_mms(*pend_mm)
             if pend_tail is not None:
                r12 = a_tail(*pend_tail)
             n_r1_, n_r2_ = a_tail(*new_tail)
             n_mu, n_rstd = ln_chain_nodes(
                g, S1[:, t0:t1_], S2[:, t0:t1_], MUA[:, t0:t1_], RSA[:, t0:t1_],
                stats[:, 256:256 + QT], stats[:, 288:288 + QT],
                [n_r1_, n_r2_], cpool[:, 0:1])
             g.lower()

             g = Graph(nc, sems, counts)
             prev = {}

             def b_tail(t, b, b2, mm0, mm1, n_lx):
                n_lh = g.dma("scalar", ht[b], h_in[:, t, :],
                             [prev.get(("ht_rd", b))])
                n_r = [None, None]
                for h, mmh in ((0, mm0), (1, mm1)):
                    n_r[h] = g.add("vector", lambda E, h=h, b=b, b2=b2:
                                   E.tensor_add(
                        out_sb[b2][:, h * 512:(h + 1) * 512], ps_y[b2 * 2 + h],
                        ht[b][:, h * 512:(h + 1) * 512]),
                        [mmh, n_lh, prev.get(("o_rd", b2))])
                prev[("y_ev", b2)] = n_r[1]
                prev[("ht_rd", b)] = n_r[1]
                st_deps = [n_r[0], n_r[1]]
                if last:
                    n_r2 = g.add("vector", lambda E, b2=b2: E.tensor_add(
                        out_sb[b2], out_sb[b2], sqdump[b2]),
                        [n_r[0], n_r[1], n_lx])
                    prev[("xf_rd", b2)] = n_r2
                    st_deps = [n_r2]
                prev[("o_rd", b2)] = g.dma(
                    "gpsimd", h_out[:, t, :], out_sb[b2], st_deps)

             def b_mms(t, b, b2, n_ev):
                n_lx = None
                if last:
                    n_lx = g.dma("scalar", sqdump[b2], xin_t[:, t, :],
                                 [prev.get(("xf_rd", b2))])
                n_lt = None
                for c in range(NC):
                    n_lt = g.add("tensor", lambda E, c=c, b=b, b2=b2, i1=i1: E.matmul(
                        ps_lt[b2][0:32, 0:128], laT[i1][:, c * 32:(c + 1) * 32],
                        hf[b][:, c * 128:(c + 1) * 128],
                        start=(c == 0), stop=(c == NC - 1)),
                        [n_ev, prev.get(("lt_ev", b2))] if c == 0 else [])
                n_lte = g.add("scalar", lambda E, b=b, b2=b2: E.copy(
                    tT_bf[b], ps_lt[b2][0:32, 0:128]),
                    [n_lt, prev.get(("tT_rd", b))])
                prev[("lt_ev", b2)] = n_lte
                mm_last = [None, None]
                for c in range(NC):
                    for h in range(2):
                        dep0 = [n_ev, prev.get(("y_ev", b2))] if c == 0 else []
                        mm_last[h] = g.add("tensor", lambda E, c=c, h=h, b=b, b2=b2:
                                           E.matmul(
                            ps_y[b2 * 2 + h], hf[b][:, c * 128:(c + 1) * 128],
                            wdqT[1][:, c * 1024 + h * 512: c * 1024 + (h + 1) * 512],
                            start=(c == 0), stop=False), dep0)
                prev[("hf_rd", b)] = mm_last[1]
                for h in range(2):
                    mm_last[h] = g.add("tensor", lambda E, h=h, i1=i1, b=b, b2=b2: E.matmul(
                        ps_y[b2 * 2 + h], ones_k1, bias_b[i1][:, h * 512:(h + 1) * 512],
                        start=False, stop=False), [n_bias[i1]] if h == 0 else [])
                for h in range(2):
                    mm_last[h] = g.add("tensor", lambda E, h=h, i1=i1, b=b, b2=b2: E.matmul(
                        ps_y[b2 * 2 + h], tT_bf[b], lbT[i1][0:32, h * 512:(h + 1) * 512],
                        start=False, stop=True), [n_lte] if h == 0 else [])
                prev[("tT_rd", b)] = mm_last[1]
                return (t, b, b2, mm_last[0], mm_last[1], n_lx)

             pend_mm = None
             pend_tail = None
             for t in range(t0, t1_):
                b = t % 4
                b2 = t % 2
                i = t - t0
                n_t1 = g.add("vector", lambda E, b2=b2, i=i, t=t: E.tensor_scalar(
                    g_sb[b2], y_big[:, i * 1024:(i + 1) * 1024],
                    MUA[:, t:t + 1], RSA[:, t:t + 1],
                    ALU.subtract, ALU.mult), [prev.get(("g_rd", b2))])
                n_t2 = g.add("vector", lambda E, b2=b2: E.tensor_mul(
                    g_sb[b2], g_sb[b2], gam_b), [n_t1, n_gam])
                n_t3 = g.add("vector", lambda E, b2=b2: E.tensor_add(
                    g_sb[b2], g_sb[b2], bet_b), [n_t2, n_bet])
                n_gl = g.add("scalar", lambda E, b2=b2: E.activation(
                    hbf[b2], g_sb[b2], AF.Gelu), [n_t3, prev.get(("hbf_rd", b2))])
                prev[("g_rd", b2)] = n_gl
                n_ev = transpose_tile_nodes(
                    g, [ps_tr, ps_tr2][b2], hbf[b2], hf[b], ident_bf, [n_gl],
                    evac_eng="scalar", bank_free=prev.get(("tr_bank", b2)),
                    evac_deps=[prev.get(("hf_rd", b))])
                prev[("hbf_rd", b2)] = n_ev
                prev[("tr_bank", b2)] = n_ev
                if pend_mm is not None:
                    new_tail = b_mms(*pend_mm)
                    if pend_tail is not None:
                        b_tail(*pend_tail)
                    pend_tail = new_tail
                pend_mm = (t, b, b2, n_ev)
             new_tail = b_mms(*pend_mm)
             if pend_tail is not None:
                b_tail(*pend_tail)
             b_tail(*new_tail)
             g.add("sync", lambda E: E.nop(),
                   [prev.get(("o_rd", bb)) for bb in range(2)])
             g.lower()


def build_l0ffn_program(debug=False):
    nc = bass.Bass("TRN2", target_bir_lowering=False)
    xin = nc.dram_tensor("x", [TOK, D], F32, kind="ExternalInput").ap()
    wr = nc.dram_tensor("wr", [D], F32, kind="ExternalInput").ap()
    wb = nc.dram_tensor("wb", [D, D], F32, kind="ExternalInput").ap()
    xout = nc.dram_tensor("xo", [TOK, D], F32, kind="ExternalOutput").ap()
    dbg = None
    if debug:
        dbg = {
            "scores": nc.dram_tensor("dbg_scores", [2, 128, 16], F32,
                                     kind="ExternalOutput").ap(),
            "tau": nc.dram_tensor("dbg_tau", [2, 128, 1], F32,
                                  kind="ExternalOutput").ap(),
            "minmax": nc.dram_tensor("dbg_minmax", [2, 1, 4], F32,
                                     kind="ExternalOutput").ap(),
            "xn": nc.dram_tensor("dbg_xn", [2, 128, 1024], F32,
                                 kind="ExternalOutput").ap(),
            "xf": nc.dram_tensor("dbg_xf", [2, 128, 1024], BF16,
                                 kind="ExternalOutput").ap(),
            "p": nc.dram_tensor("dbg_p", [2, 128, 1024], F32,
                                kind="ExternalOutput").ap(),
        }
    with ExitStack() as st:
        sems, counts = make_sems(nc, st)
        emit_l0ffn(nc, sems, counts, xin, xout, wr, wb, dbg=dbg)
    return nc


def build_l0attn_program():
    nc = bass.Bass("TRN2", target_bir_lowering=False)
    xin = nc.dram_tensor("x", [TOK, D], F32, kind="ExternalInput").ap()
    wq = nc.dram_tensor("wq", [D, D], F32, kind="ExternalInput").ap()
    wk = nc.dram_tensor("wk", [D, D], F32, kind="ExternalInput").ap()
    wv = nc.dram_tensor("wv", [D, D], F32, kind="ExternalInput").ap()
    wo = nc.dram_tensor("wo", [D, D], F32, kind="ExternalInput").ap()
    xout = nc.dram_tensor("xo", [TOK, D], F32, kind="ExternalOutput").ap()
    with ExitStack() as st:
        sems, counts = make_sems(nc, st)
        emit_l0attn(nc, sems, counts, xin, xout, wq, wk, wv, wo)
    return nc


def build_l1attn_program():
    nc = bass.Bass("TRN2", target_bir_lowering=False)
    xin = nc.dram_tensor("x", [TOK, D], F32, kind="ExternalInput").ap()
    wqkv = nc.dram_tensor("wqkv", [D, 3 * D], F32, kind="ExternalInput").ap()
    xout = nc.dram_tensor("xo", [TOK, D], F32, kind="ExternalOutput").ap()
    with ExitStack() as st:
        sems, counts = make_sems(nc, st)
        emit_l1attn(nc, sems, counts, xin, xout, wqkv)
    return nc


def build_l1ffn_program():
    nc = bass.Bass("TRN2", target_bir_lowering=False)
    xin = nc.dram_tensor("x", [TOK, D], F32, kind="ExternalInput").ap()
    qw = nc.dram_tensor("qw", [6, D, D], I32, kind="ExternalInput").ap()
    qs = nc.dram_tensor("qs", [6, D, 64], F32, kind="ExternalInput").ap()
    qb = nc.dram_tensor("qb", [6, D], F32, kind="ExternalInput").ap()
    la = nc.dram_tensor("la", [6, 32, D], F32, kind="ExternalInput").ap()
    lb = nc.dram_tensor("lb", [6, D, 32], F32, kind="ExternalInput").ap()
    lg = nc.dram_tensor("lg", [3, D], F32, kind="ExternalInput").ap()
    lbe = nc.dram_tensor("lbe", [3, D], F32, kind="ExternalInput").ap()
    xout = nc.dram_tensor("xo", [TOK, D], F32, kind="ExternalOutput").ap()
    scratch = {k: nc.dram_tensor("scr_" + k, [TOK, D], F32, kind="Internal").ap()
               for k in ("h0", "h1", "h2", "y1")}
    with ExitStack() as st:
        sems, counts = make_sems(nc, st)
        emit_l1ffn(nc, sems, counts, xin, xout, qw, qs, qb, la, lb, lg, lbe,
                   scratch)
    return nc


def build_fused_program():
    nc = bass.Bass("TRN2", target_bir_lowering=False)
    xin = nc.dram_tensor("x", [TOK, D], F32, kind="ExternalInput").ap()
    wq = nc.dram_tensor("wq", [D, D], F32, kind="ExternalInput").ap()
    wk = nc.dram_tensor("wk", [D, D], F32, kind="ExternalInput").ap()
    wv = nc.dram_tensor("wv", [D, D], F32, kind="ExternalInput").ap()
    wo = nc.dram_tensor("wo", [D, D], F32, kind="ExternalInput").ap()
    wr = nc.dram_tensor("wr", [D], F32, kind="ExternalInput").ap()
    wb = nc.dram_tensor("wb", [D, D], F32, kind="ExternalInput").ap()
    wqkv = nc.dram_tensor("wqkv", [D, 3 * D], F32, kind="ExternalInput").ap()
    qw = nc.dram_tensor("qw", [6, D, D], I32, kind="ExternalInput").ap()
    qs = nc.dram_tensor("qs", [6, D, 64], F32, kind="ExternalInput").ap()
    qb = nc.dram_tensor("qb", [6, D], F32, kind="ExternalInput").ap()
    la = nc.dram_tensor("la", [6, 32, D], F32, kind="ExternalInput").ap()
    lb = nc.dram_tensor("lb", [6, D, 32], F32, kind="ExternalInput").ap()
    lg = nc.dram_tensor("lg", [3, D], F32, kind="ExternalInput").ap()
    lbe = nc.dram_tensor("lbe", [3, D], F32, kind="ExternalInput").ap()
    xout = nc.dram_tensor("xo", [TOK, D], F32, kind="ExternalOutput").ap()
    xa = nc.dram_tensor("scr_xa", [TOK, D], F32, kind="Internal").ap()
    xb = nc.dram_tensor("scr_xb", [TOK, D], F32, kind="Internal").ap()
    scratch = {k: nc.dram_tensor("scr_" + k, [TOK, D], F32, kind="Internal").ap()
               for k in ("h0", "h1", "h2", "y1")}
    with ExitStack() as st:
        sems, counts = make_sems(nc, st)
        emit_l0attn(nc, sems, counts, xin, xa, wq, wk, wv, wo)
        emit_l0ffn(nc, sems, counts, xa, xb, wr, wb)
        emit_l1attn(nc, sems, counts, xb, xa, wqkv)
        emit_l1ffn(nc, sems, counts, xa, xout, qw, qs, qb, la, lb, lg, lbe,
                   scratch)
    return nc


_PROGRAMS = {}


def _get_program(name):
    if name not in _PROGRAMS:
        _PROGRAMS[name] = {
            "l0attn": build_l0attn_program,
            "l0ffn": build_l0ffn_program,
            "l1attn": build_l1attn_program,
            "l1ffn": build_l1ffn_program,
            "fused": build_fused_program,
        }[name]()
    return _PROGRAMS[name]


def kernel(**inputs):
    x = np.ascontiguousarray(np.asarray(inputs["L0_attn_x"], np.float32))
    w = {k: np.ascontiguousarray(np.asarray(v)) for k, v in inputs.items()
         if k != "L0_attn_x"}
    shards = [np.ascontiguousarray(x[2 * c:2 * c + 2].reshape(TOK, D))
              for c in range(8)]
    res = run_bass_kernel_spmd(
        _get_program("fused"),
        [{"x": s, "wq": w["L0_attn_Wq"], "wk": w["L0_attn_Wk"],
          "wv": w["L0_attn_Wv"], "wo": w["L0_attn_Wo"],
          "wr": w["L0_ffn_w_router"], "wb": w["L0_ffn_w_block"],
          "wqkv": w["L1_attn_W_qkv"], "qw": w["L1_ffn_qweights"],
          "qs": w["L1_ffn_scales"], "qb": w["L1_ffn_biases"],
          "la": w["L1_ffn_lora_a"], "lb": w["L1_ffn_lora_b"],
          "lg": w["L1_ffn_ln_gamma"], "lbe": w["L1_ffn_ln_beta"]}
         for s in shards],
        core_ids=list(range(8)))
    return np.concatenate([r["xo"].reshape(2, T, D) for r in res.results],
                          axis=0)
